# version 18
# baseline (speedup 1.0000x reference)
import numpy as np
import concourse.bass as bass
import concourse.mybir as mybir
from concourse.bass_utils import run_bass_kernel_spmd

F32 = mybir.dt.float32
BF16 = mybir.dt.bfloat16
U32 = mybir.dt.uint32
AF = mybir.ActivationFunctionType
ALU = mybir.AluOpType

D, H, HD, NB = 1024, 16, 64, 16
NE_R, NE_S = 4, 2
NEX = NE_S + NE_R
NDC = D // 128
ALPHA = 64.0
CBIG = 65536.0
EPS = 1e-5
DBG = False
X1OUT = False
DBGJ = -1
DBG_EXPERTS = []
DBG_YEX = -1

LANES = ("T", "V", "SC", "SD", "G", "GD", "SY")
_SECTION = {"T": "tensor", "V": "vector", "SC": "scalar", "SD": "scalar",
            "G": "gpsimd", "GD": "gpsimd", "SY": "sync"}
_DMA_LANES = {"SD", "GD", "SY"}


class Cfg:
    def __init__(self, TF=4096, SW=512, NSTR=4, FF=4096, FB=8):
        self.TF, self.SW, self.NSTR, self.FF, self.FB = TF, SW, NSTR, FF, FB
        self.TO = SW * NSTR
        self.NTAU = TF // SW
        self.NTC = SW // 128
        self.NFC = FF // 128

    def rng_chunks(self, j):
        return min((2 * j + 2) * self.SW // 128, self.TF // 128)


class Plan:

    def __init__(self, nc):
        self.nc = nc
        self.steps = []
        self.cnt = {}
        self.sem = {}
        self.last_w = {}
        self.readers = {}

    DRAM_BASES = {"KA_d", "KB_d", "V_d", "QA_d", "QB_d", "x1_d", "y_dram",
                  "dbgQ", "dbgQB", "dbgO", "dbgC", "dbgP", "dbgG", "dbgCB",
                  "dbgH", "dbgW2", "dbgY"}

    def _semkey(self, lane, reads, writes):
        if lane in _DMA_LANES:
            tgt = writes[0] if writes else "misc"
            if tgt in self.DRAM_BASES and reads:
                tgt = reads[0]
            return "D" + lane + "_" + tgt
        return lane

    def _get_sem(self, key):
        if key not in self.sem:
            self.sem[key] = self.nc.alloc_semaphore("S_" + key)
            self.cnt[key] = 0
        return self.sem[key]

    @staticmethod
    def _parse(name):
        if "#" in name:
            b, s = name.split("#", 1)
            return b, s
        return name, "*"

    def step(self, lane, emit, reads=(), writes=(), tag=""):
        key = self._semkey(lane, [self._parse(r)[0] for r in reads],
                           [self._parse(w)[0] for w in writes])
        self._get_sem(key)
        waits = {}
        drain = [False]

        def need(dep):
            dl, dkey, dcnt = dep
            if dl == lane and lane not in _DMA_LANES and dkey == lane:
                if lane != "T":
                    drain[0] = True
            else:
                if dkey.startswith("D"):
                    dcnt = self.cnt[dkey]
                waits[dkey] = max(waits.get(dkey, 0), dcnt)

        def sufs_conflicting(d, s):
            if s == "*":
                return list(d.keys())
            out = [t for t in (s, "*") if t in d]
            return out

        for r in reads:
            b, s = self._parse(r)
            d = self.last_w.get(b, {})
            for t in sufs_conflicting(d, s):
                need(d[t])
        for wn in writes:
            b, s = self._parse(wn)
            d = self.last_w.get(b, {})
            for t in sufs_conflicting(d, s):
                need(d[t])
            rd = self.readers.get(b, {})
            for t in sufs_conflicting(rd, s):
                for (rl, rkey), rc in rd[t].items():
                    need((rl, rkey, rc))
        inc = 16 if lane in _DMA_LANES else 1
        self.cnt[key] += inc
        mc = self.cnt[key]
        for wn in writes:
            b, s = self._parse(wn)
            d = self.last_w.setdefault(b, {})
            if s == "*":
                d.clear()
            d[s] = (lane, key, mc)
            rd = self.readers.setdefault(b, {})
            if s == "*":
                rd.clear()
            else:
                rd.pop(s, None)
                rd.pop("*", None)
        for r in reads:
            b, s = self._parse(r)
            self.readers.setdefault(b, {}).setdefault(s, {})[(lane, key)] = mc
        self.steps.append(dict(lane=lane, key=key, emit=emit, waits=waits,
                               drain=drain[0], inc=inc, tag=tag))

    def barrier(self):
        self.steps.append(dict(lane=None, waits=dict(self.cnt), tag="barrier"))

    def emit(self, block):
        seen = {ln: {} for ln in LANES}
        pend = {ln: {} for ln in LANES}
        merged = {}
        for s in self.steps:
            if s["lane"] is None:
                for ln in LANES:
                    for k, c in s["waits"].items():
                        if k != ln and c > 0:
                            pend[ln][k] = max(pend[ln].get(k, 0), c)
                continue
            ln = s["lane"]
            waits = dict(pend[ln])
            pend[ln] = {}
            for k, c in s["waits"].items():
                waits[k] = max(waits.get(k, 0), c)
            eff = {}
            for k, c in waits.items():
                if k == ln:
                    continue
                if c > seen[ln].get(k, 0):
                    eff[k] = c
                    seen[ln][k] = c
            merged.setdefault(_SECTION[ln], []).append(
                (ln, s["key"], eff, s["drain"], s["emit"], s["inc"], s["tag"]))

        for sec, items in merged.items():
            def body(raw, items=items):
                for ln, key, eff, drain, emitfn, inc, tag in items:
                    for k, c in eff.items():
                        raw.wait_ge(self.sem[k], c)
                    if drain:
                        raw.drain()
                    last = emitfn(raw)
                    assert last is not None, f"step {tag!r} on {ln} returned None"
                    last.then_inc(self.sem[key], inc)
            getattr(block, sec)(body)


def build_program(cfg: Cfg):
    nc = bass.Bass("TRN2", target_bir_lowering=False)
    TF, SW, NSTR, FF, FB = cfg.TF, cfg.SW, cfg.NSTR, cfg.FF, cfg.FB
    TO, NTAU, NTC, NFC = cfg.TO, cfg.NTAU, cfg.NTC, cfg.NFC
    NOT = D // 512
    NFB = NFC // FB

    def I(name, shape, dt=F32):
        return nc.dram_tensor(name, shape, dt, kind="ExternalInput").ap()

    x_full = I("x_full", [TF, D])
    x_own = I("x_own", [TO, D])
    qpos_d = I("qpos", [1, TO])
    wq_d = I("wq", [D, D]); wk_d = I("wk", [D, D])
    wv_d = I("wv", [D, D]); wo_d = I("wo", [D, D])
    wh_d = I("wh", [D, NB])
    gw_d = I("gate_w", [D, NE_R]); gb_d = I("gate_b", [1, NE_R])
    sw1_d = I("sw1", [NE_S, D, FF]); sb1_d = I("sb1", [NE_S, FF])
    sw2_d = I("sw2", [NE_S, FF, D]); sb2_d = I("sb2", [NE_S, D])
    rw1_d = I("rw1", [NE_R, D, FF]); rb1_d = I("rb1", [NE_R, FF])
    rw2_d = I("rw2", [NE_R, FF, D]); rb2_d = I("rb2", [NE_R, D])
    y_d = nc.dram_tensor("y", [TO, D], F32, kind="ExternalOutput").ap()

    KA_d = nc.dram_tensor("KA_d", [H, 144, TF], BF16).ap()
    V_d = nc.dram_tensor("V_d", [H, TF, 130], BF16).ap()
    x1_d = nc.dram_tensor("x1_d", [TO, D], F32,
                          kind=("ExternalOutput" if X1OUT else "Internal")).ap()
    QA_d = nc.dram_tensor("QA_d", [H, 144, TO], BF16).ap()
    dbgQ = nc.dram_tensor("dbgQ", [128, H * SW], mybir.dt.bfloat16).ap()
    dbgQB = nc.dram_tensor("dbgQB", [128, H * SW], mybir.dt.bfloat16).ap()
    dbgO = nc.dram_tensor("dbgO", [H, 64, SW], F32).ap()
    dbgC = nc.dram_tensor("dbgC", [128, 8 * SW], F32).ap()
    dbgP = nc.dram_tensor("dbgP", [128, SW + 8], F32).ap()
    dbgG = nc.dram_tensor("dbgG", [128, 16], F32).ap()
    dbgCB = nc.dram_tensor("dbgCB", [4, 128, SW], F32).ap()
    dbgH = nc.dram_tensor("dbgH", [2, 128, SW], mybir.dt.bfloat16).ap()
    dbgW2 = nc.dram_tensor("dbgW2", [2, 128, 4096], mybir.dt.bfloat16).ap()
    dbgY = nc.dram_tensor("dbgY", [128, 512], F32).ap()
    dbgYs = nc.alloc_sbuf_tensor("dbgYs", [128, 512], F32).ap() if DBG else None

    P = Plan(nc)
    SB = lambda n, sh, dt=F32: nc.alloc_sbuf_tensor(n, sh, dt).ap()

    identbf = SB("identbf", [128, 128], BF16)
    iotaF = SB("iotaF", [128, 128])
    kcol = SB("kcol", [128, 1])
    kcc = SB("kcc", [128, 1])
    ones1 = SB("ones1", [1, 128], BF16)
    nalpha = SB("nalpha", [128, 1])
    qpos_hi = SB("qpos_hi", [1, TO], BF16)
    qpos_lo = SB("qpos_lo", [1, TO], BF16)
    wh_32 = SB("wh_32", [128, NDC * NB])
    wh_hi = SB("wh_hi", [128, NDC * NB], BF16)
    wh_lo = SB("wh_lo", [128, NDC * NB], BF16)
    gw_32 = SB("gw_32", [128, NDC * NE_R])
    gw_hi = SB("gw_hi", [128, NDC * NE_R], BF16)
    gw_lo = SB("gw_lo", [128, NDC * NE_R], BF16)
    gb_32 = SB("gb_32", [1, NE_R])
    gb_hi = SB("gb_hi", [1, NE_R], BF16)
    gb_lo = SB("gb_lo", [1, NE_R], BF16)
    b1r_32 = None
    b1r_hi = None
    b1r_lo = None
    b1cols = SB("b1cols", [128, NEX * NFC])

    b2_hi = SB("b2_hi", [NEX, D], BF16)
    b2_lo = SB("b2_lo", [NEX, D], BF16)
    dummy = SB("kdummy", [1, 8])

    AR1 = SB("AR1", [128, 8192], BF16)
    AR2 = SB("AR2", [128, 8192], BF16)
    AR3 = SB("AR3", [128, 8192], BF16)
    AR4 = SB("AR4", [128, 8192], BF16)
    AR5 = SB("AR5", [128, 4096])
    AR6 = SB("AR6", [128, 4096])
    AR7 = SB("AR7", [128, 4096])

    b1r_32 = AR5[0:NEX, 0:FF]
    b1r_hi = AR1[0:NEX, 0:FF]
    b1r_lo = AR2[0:NEX, 0:FF]
    qpos_sb = AR6[0:1, 0:TO]
    b2_32 = AR6[0:NEX, 2048:2048 + D]

    xfh = SB("xfh", [128, NDC * SW], BF16)
    xfl = SB("xfl", [128, NDC * SW], BF16)
    xfhB = AR7[:, 0:NDC * SW // 2].bitcast(BF16)
    xflB = AR7[:, NDC * SW // 2:NDC * SW].bitcast(BF16)
    XF = [(xfh, xfl, "xfh", "xfl"), (xfhB, xflB, "xfhB", "xflB")]
    xst = SB("xst", [128, D])
    xsq = SB("xsq", [128, D])
    xhh = [SB(f"xhh{i}", [128, D], BF16) for i in range(2)]
    xhl = [SB(f"xhl{i}", [128, D], BF16) for i in range(2)]
    mcol = SB("mcol", [128, 8])
    gcol = SB("gcol", [128, 2])

    NKST = 4
    kh_st = [SB(f"kh_st{i}", [128, SW], BF16) for i in range(NKST)]
    kl_st = [SB(f"kl_st{i}", [128, SW], BF16) for i in range(NKST)]
    vst = [SB(f"vst{i}", [128, H * 130], BF16) for i in range(2)]
    lg32 = SB("lg32", [128, NTC * NB])
    m8 = SB("m8", [128, 8])
    mi8 = SB("mi8", [128, 8], U32)
    mif = SB("mif", [128, 8])
    u8t = SB("u8t", [128, NB], BF16)
    u8f = SB("u8f", [16, SW], BF16)
    u8own = SB("u8own", [16, SW], BF16)

    psA = nc.alloc_psum_tensor("psA", [128, 512], F32).ap()
    psB = nc.alloc_psum_tensor("psB", [128, 512], F32).ap()
    psC = nc.alloc_psum_tensor("psC", [128, 512], F32).ap()
    psD = nc.alloc_psum_tensor("psD", [128, 512], F32).ap()
    psO = nc.alloc_psum_tensor("psO", [128, 512], F32).ap()
    psT = nc.alloc_psum_tensor("psT", [128, 1024], BF16).ap()

    def dma(lane, dst, src, reads, writes, tag=""):
        P.step(lane, lambda raw, d=dst, s=src: raw.dma_start(out=d, in_=s),
               reads=reads, writes=writes, tag=tag)

    dma("SY", qpos_sb, qpos_d, ["i"], ["qpos_sb"], "p0 qpos")
    dma("SY", gb_32, gb_d, ["i"], ["gb_32"], "p0 gb")
    dma("SY", wh_32.rearrange("p (dc n) -> p dc n", dc=NDC),
        wh_d.rearrange("(dc p) n -> p dc n", p=128), ["i"], ["wh_32"], "p0 wh")
    dma("SY", gw_32.rearrange("p (dc n) -> p dc n", dc=NDC),
        gw_d.rearrange("(dc p) n -> p dc n", p=128), ["i"], ["gw_32"], "p0 gw")
    dma("SY", b1r_32[0:NE_S, :], sb1_d, ["i"], ["b1r_32"], "p0 sb1")
    dma("SY", b1r_32[NE_S:NEX, :], rb1_d, ["i"], ["b1r_32"], "p0 rb1")
    dma("SY", b2_32[NE_R:NEX, :], sb2_d, ["i"], ["b2_32"], "p0 sb2")
    dma("SY", b2_32[0:NE_R, :], rb2_d, ["i"], ["b2_32"], "p0 rb2")

    P.step("G", lambda g: g.iota(iotaF, pattern=[[1, 128]], base=0,
                                 channel_multiplier=0,
                                 allow_small_or_imprecise_dtypes=True),
           writes=["iotaF"], tag="p0 iotaF")
    P.step("G", lambda g: g.iota(kcol, pattern=[[1, 1]], base=0,
                                 channel_multiplier=1,
                                 allow_small_or_imprecise_dtypes=True),
           writes=["kcol"], tag="p0 kcol")

    def mk_ident(v):
        v.tensor_scalar(xst[:, 0:128], iotaF, kcol[:, 0:1], None, ALU.subtract)
        v.drain()
        return v.tensor_scalar(identbf, xst[:, 0:128], 0.0, None, ALU.is_equal)
    P.step("V", mk_ident, reads=["iotaF", "kcol"], writes=["identbf", "xst"],
           tag="p0 ident")
    P.step("V", lambda v: v.memset(ones1, 1.0), writes=["ones1"], tag="p0 ones")
    P.step("V", lambda v: v.memset(nalpha, -ALPHA), writes=["nalpha"], tag="p0 na")
    P.step("V", lambda v: v.tensor_scalar(b1r_32[0:NE_S, :], b1r_32[0:NE_S, :],
                                          0.5, None, ALU.mult),
           reads=["b1r_32"], writes=["b1r_32"], tag="p0 b1half")

    for nm, w32, whi, wlo in (("wh", wh_32, wh_hi, wh_lo),
                              ("gw", gw_32, gw_hi, gw_lo),
                              ("gb", gb_32, gb_hi, gb_lo),
                              ("qpos", qpos_sb, qpos_hi, qpos_lo),
                              ("b1r", b1r_32, b1r_hi, b1r_lo),
                              ("b2", b2_32, b2_hi, b2_lo)):
        srcn = {"wh": "wh_32", "gw": "gw_32", "gb": "gb_32",
                "qpos": "qpos_sb", "b1r": "b1r_32", "b2": "b2_32"}[nm]
        P.step("V", lambda v, a=w32, b=whi: v.tensor_copy(b, a),
               reads=[srcn], writes=[nm + "_hi"], tag=f"p0 {nm} hi")
        P.step("V", lambda v, a=w32, b=whi, c=wlo:
               v.scalar_tensor_tensor(c, a, 1.0, b, ALU.mult, ALU.subtract),
               reads=[srcn, nm + "_hi"], writes=[nm + "_lo"], tag=f"p0 {nm} lo")

    b1cv = b1cols.rearrange("p (e f) -> p e f", e=NEX)
    for fc in range(NFC):
        fsl = slice(fc * 128, (fc + 1) * 128)
        P.step("T", lambda te, fsl=fsl: te.transpose(
            psT[0:128, 0:NEX], b1r_hi[:, fsl], identbf[0:NEX, 0:NEX]),
            reads=["b1r_hi", "identbf", "psT"], writes=["psT"],
            tag=f"p0 b1t {fc}")
        P.step("V", lambda v, fc=fc: v.tensor_copy(
            b1cv[:, :, fc], psT[0:128, 0:NEX]),
            reads=["psT"], writes=["b1cols"], tag=f"p0 b1e {fc}")
        P.step("T", lambda te, fsl=fsl: te.transpose(
            psT[0:128, 0:NEX], b1r_lo[:, fsl], identbf[0:NEX, 0:NEX]),
            reads=["b1r_lo", "identbf", "psT"], writes=["psT"],
            tag=f"p0 b1t2 {fc}")
        P.step("V", lambda v, fc=fc: v.tensor_tensor(
            b1cv[:, :, fc], b1cv[:, :, fc], psT[0:128, 0:NEX], ALU.add),
            reads=["psT", "b1cols"], writes=["b1cols"], tag=f"p0 b1a {fc}")

    P.barrier()

    def ln_stats_part(src_d, src_name, row0, tag, tc):
        hb = tc % 2
        rows = slice(row0 + tc * 128, row0 + (tc + 1) * 128)
        dma("SY", xst, src_d[rows, :], [src_name], ["xst"], f"{tag} x{tc}")
        P.step("V", lambda v: v.tensor_reduce(
            mcol[:, 0:1], xst, mybir.AxisListType.X, ALU.add),
            reads=["xst"], writes=["mcol"], tag=f"{tag} sum{tc}")
        P.step("SC", lambda s: s.activation(
            xsq, xst, AF.Square, accum_out=mcol[:, 1:2]),
            reads=["xst"], writes=["xsq", "mcol"], tag=f"{tag} sq{tc}")

        def stats(v):
            v.tensor_scalar(mcol[:, 0:1], mcol[:, 0:1], 1.0 / D, None, ALU.mult)
            v.drain()
            v.tensor_scalar(mcol[:, 2:3], mcol[:, 1:2], 1.0 / D, None, ALU.mult)
            v.drain()
            v.tensor_tensor(mcol[:, 3:4], mcol[:, 0:1], mcol[:, 0:1], ALU.mult)
            v.drain()
            v.tensor_tensor(mcol[:, 2:3], mcol[:, 2:3], mcol[:, 3:4], ALU.subtract)
            v.drain()
            v.tensor_scalar(mcol[:, 2:3], mcol[:, 2:3], EPS, None, ALU.add)
            v.drain()
            return v.reciprocal(mcol[:, 4:5], mcol[:, 2:3])
        P.step("V", stats, reads=["mcol"], writes=["mcol"], tag=f"{tag} st{tc}")
        P.step("SC", lambda s: s.activation(mcol[:, 5:6], mcol[:, 4:5], AF.Sqrt),
               reads=["mcol"], writes=["mcol"], tag=f"{tag} rs{tc}")
        P.step("V", lambda v: v.tensor_scalar(
            xsq, xst, mcol[:, 0:1], mcol[:, 5:6], ALU.subtract, ALU.mult),
            reads=["xst", "mcol", "xsq"], writes=["xsq"], tag=f"{tag} xh{tc}")
        P.step("V", lambda v, hb=hb: v.tensor_copy(xhh[hb], xsq),
               reads=["xsq"], writes=[f"xhh{hb}"], tag=f"{tag} hh{tc}")
        P.step("V", lambda v, hb=hb: v.scalar_tensor_tensor(
            xhl[hb], xsq, 1.0, xhh[hb], ALU.mult, ALU.subtract),
            reads=["xsq", f"xhh{hb}"], writes=[f"xhl{hb}"], tag=f"{tag} hl{tc}")

    def ln_transpose_part(tag, tc, buf):
        hb = tc % 2
        xfh_t, xfl_t, nxh, nxl = XF[buf]

        def trh(te, hb=hb):
            last = None
            for dc in range(NDC):
                last = te.transpose(psT[:, dc * 128:(dc + 1) * 128],
                                    xhh[hb][:, dc * 128:(dc + 1) * 128], identbf)
            return last
        P.step("T", trh, reads=[f"xhh{hb}", "identbf", "psT"], writes=["psT"],
               tag=f"{tag} th{tc}")
        P.step("SC", lambda s, tc=tc, xf=xfh_t: s.activation(
            xf.rearrange("p (dc t) -> p dc t", dc=NDC)[:, :, tc * 128:(tc + 1) * 128],
            psT.rearrange("p (dc t) -> p dc t", dc=NDC), AF.Copy),
            reads=["psT"], writes=[nxh], tag=f"{tag} eh{tc}")

        def trl(te, hb=hb):
            last = None
            for dc in range(NDC):
                last = te.transpose(psT[:, dc * 128:(dc + 1) * 128],
                                    xhl[hb][:, dc * 128:(dc + 1) * 128], identbf)
            return last
        P.step("T", trl, reads=[f"xhl{hb}", "identbf", "psT"], writes=["psT"],
               tag=f"{tag} tl{tc}")
        P.step("SC", lambda s, tc=tc, xf=xfl_t: s.activation(
            xf.rearrange("p (dc t) -> p dc t", dc=NDC)[:, :, tc * 128:(tc + 1) * 128],
            psT.rearrange("p (dc t) -> p dc t", dc=NDC), AF.Copy),
            reads=["psT"], writes=[nxl], tag=f"{tag} el{tc}")

    def ln_split_transpose(src_d, src_name, row0, tag, buf=0):
        for tc in range(NTC):
            ln_stats_part(src_d, src_name, row0, tag, tc)
            ln_transpose_part(tag, tc, buf)

    def k_proj_part(oc, whi_t, wlo_t, whin, wlon, tag, buf=0):
        b = oc % NKST
        ps, psn = (psA, "psA") if b == 0 else (psB, "psB")
        xfh_t, xfl_t, nxh, nxl = XF[buf]

        def kproj(te, oc=oc, ps=ps):
            last = None
            for dc in range(NDC):
                osl = slice(dc * D + oc * 128, dc * D + oc * 128 + 128)
                tsl = slice(dc * SW, (dc + 1) * SW)
                last = te.matmul(ps[:, 0:SW], whi_t[:, osl], xfh_t[:, tsl],
                                 start=(dc == 0), stop=False)
                last = te.matmul(ps[:, 0:SW], whi_t[:, osl], xfl_t[:, tsl],
                                 start=False, stop=False)
                last = te.matmul(ps[:, 0:SW], wlo_t[:, osl], xfh_t[:, tsl],
                                 start=False, stop=(dc == NDC - 1))
            return last
        P.step("T", kproj, reads=[whin, wlon, nxh, nxl], writes=[psn],
               tag=f"{tag} proj{oc}")
        P.step("SC", lambda s, ps=ps, b=b: s.activation(
            kh_st[b], ps[:, 0:SW], AF.Copy),
            reads=[psn], writes=[f"kh_st{b}"], tag=f"{tag} ev{oc}")
        P.step("V", lambda v, ps=ps, b=b: v.scalar_tensor_tensor(
            kl_st[b], ps[:, 0:SW], 1.0, kh_st[b], ALU.mult, ALU.subtract),
            reads=[psn, f"kh_st{b}"], writes=[f"kl_st{b}"], tag=f"{tag} lo{oc}")

    def k_log_part(oc, tag):
        b = oc % NKST

        def klog(te, oc=oc, b=b):
            last = None
            for tc in range(NTC):
                tsl = slice(tc * 128, (tc + 1) * 128)
                lsl = slice(tc * NB, (tc + 1) * NB)
                whh = wh_hi[:, oc * NB:(oc + 1) * NB]
                whl = wh_lo[:, oc * NB:(oc + 1) * NB]
                last = te.matmul(psO[:, lsl], kh_st[b][:, tsl], whh,
                                 start=True, stop=False)
                last = te.matmul(psO[:, lsl], kh_st[b][:, tsl], whl,
                                 start=False, stop=False)
                last = te.matmul(psO[:, lsl], kl_st[b][:, tsl], whh,
                                 start=False, stop=True)
            return last
        P.step("T", klog, reads=[f"kh_st{b}", f"kl_st{b}", "wh_hi", "wh_lo"],
               writes=["psO"], tag=f"{tag} lg{oc}")
        if oc == 0:
            P.step("SC", lambda s: s.activation(lg32, psO[:, 0:NTC * NB],
                                                AF.Copy),
                   reads=["psO"], writes=["lg32"], tag=f"{tag} lga{oc}")
        else:
            P.step("V", lambda v: v.tensor_tensor(lg32, lg32,
                                                  psO[:, 0:NTC * NB], ALU.add),
                   reads=["psO", "lg32"], writes=["lg32"], tag=f"{tag} lga{oc}")

    def u_compute(u_target, u_name, tag):
        for tc in range(NTC):
            lsl = slice(tc * NB, (tc + 1) * NB)

            def ufn(v, lsl=lsl):
                v.max(m8, lg32[:, lsl])
                v.drain()
                v.max_index(mi8, m8, lg32[:, lsl])
                v.drain()
                v.tensor_copy(mif, mi8)
                v.drain()
                return v.tensor_scalar(u8t, iotaF[:, 0:NB], mif[:, 0:1], 8.0,
                                       ALU.is_equal, ALU.mult)
            P.step("V", ufn, reads=["lg32", "iotaF"],
                   writes=["u8t", "m8", "mi8", "mif"], tag=f"{tag} u{tc}")
            P.step("T", lambda te: te.transpose(psT[0:16, 0:128], u8t, identbf),
                   reads=["u8t", "identbf", "psT"], writes=["psT"],
                   tag=f"{tag} ut{tc}")
            P.step("V", lambda v, tc=tc: v.tensor_copy(
                u_target[:, tc * 128:(tc + 1) * 128], psT[0:16, 0:128]),
                reads=["psT"], writes=[u_name], tag=f"{tag} ue{tc}")

    wkhi, wklo, wvhi, wvlo = AR1, AR2, AR3, AR4

    for wi, (wd, whi_t, wlo_t, nmw) in enumerate(
            ((wk_d, wkhi, wklo, "wk"), (wv_d, wvhi, wvlo, "wv"))):
        for oc in range(NDC):
            osl = slice(oc * 128, (oc + 1) * 128)
            slot = (oc % 2) + 2 * wi
            st32 = AR5[:, slot * 1024:slot * 1024 + 1024]
            st32v = st32.rearrange("p (dc o) -> p dc o", dc=NDC)
            src = wd.rearrange("(dc p) o -> p dc o", p=128)[:, :, osl]
            dst_hi = whi_t.rearrange("p (dc o) -> p dc o", dc=NDC)[:, :, osl]
            dst_lo = wlo_t.rearrange("p (dc o) -> p dc o", dc=NDC)[:, :, osl]
            nm32 = f"w32s_{slot}"
            dma("SY", st32v, src, ["i"], [nm32], f"p1 w32 {nmw}{oc}")
            P.step("SC", lambda s, a=dst_hi, b=st32v: s.activation(
                a, b, AF.Copy), reads=[nm32, nmw + "_hi"], writes=[nmw + "_hi"],
                tag=f"p1 whi {nmw}{oc}")
            P.step("V", lambda v, a=st32v, b=dst_hi, c=dst_lo:
                   v.scalar_tensor_tensor(c, a, 1.0, b, ALU.mult, ALU.subtract),
                   reads=[nm32, nmw + "_hi"], writes=[nmw + "_lo"],
                   tag=f"p1 wlo {nmw}{oc}")

    for vb in range(2):
        vv3g = vst[vb].rearrange("p (h c) -> p h c", h=H)
        P.step("V", lambda v, a=vv3g[:, :, 64:65]: v.memset(a, 1.0),
               writes=[f"vst{vb}"], tag=f"p1 vones{vb}")
        P.step("V", lambda v, a=vv3g[:, :, 129:130]: v.memset(a, 0.0),
               writes=[f"vst{vb}"], tag=f"p1 vzer{vb}")
    ln_split_transpose(x_full, "x_full", 0, "p1.0", buf=0)
    for tau in range(NTAU):
        cols = slice(tau * SW, (tau + 1) * SW)
        buf = tau % 2

        def k_scratch_dma(oc):
            b = oc % NKST
            for hh, base in ((2 * oc, 0), (2 * oc + 1, 64)):
                sl = slice(base, base + 64)
                dma("SY", KA_d[hh, 0:64, cols], kh_st[b][sl, :],
                    [f"kh_st{b}"], [f"KA_d#{tau}.{hh}.a"], f"p1 ka {tau}.{oc}")
                dma("SY", KA_d[hh, 80:144, cols], kl_st[b][sl, :],
                    [f"kl_st{b}"], [f"KA_d#{tau}.{hh}.b"], f"p1 kb {tau}.{oc}")

        for oc in range(NDC):
            k_proj_part(oc, wkhi, wklo, "wk_hi", "wk_lo", f"p1K.{tau}", buf=buf)
            if oc > 0:
                k_log_part(oc - 1, f"p1K.{tau}")
                k_scratch_dma(oc - 1)
            if tau + 1 < NTAU:
                if oc % 2 == 1 and oc >= 3 and (oc - 2) // 2 < NTC:
                    ln_transpose_part(f"p1.{tau + 1}", (oc - 2) // 2,
                                      (tau + 1) % 2)
                if oc % 2 == 1 and oc // 2 < NTC:
                    ln_stats_part(x_full, "x_full", (tau + 1) * SW,
                                  f"p1.{tau + 1}", oc // 2)
        k_log_part(NDC - 1, f"p1K.{tau}")
        k_scratch_dma(NDC - 1)
        if tau + 1 < NTAU:
            for tcr in range(max(0, min(NTC, (NDC - 2) // 2)), NTC):
                ln_transpose_part(f"p1.{tau + 1}", tcr, (tau + 1) % 2)

        u_compute(u8f, "u8f", f"p1U.{tau}")
        for hh in range(H):
            dma("GD", KA_d[hh, 64:80, cols], u8f, ["u8f"],
                [f"KA_d#{tau}.{hh}.u"], f"p1 ua {tau}.{hh}")

        for tc in range(NTC):
            for ot in range(NOT):
                i = tc * NOT + ot
                ps, psn = (psC, "psC") if i % 2 == 0 else (psD, "psD")

                xfh_t, xfl_t, nxh, nxl = XF[buf]

                def vproj(te, tc=tc, ot=ot, ps=ps, xfh_t=xfh_t, xfl_t=xfl_t):
                    last = None
                    for dc in range(NDC):
                        xsl = slice(dc * SW + tc * 128, dc * SW + tc * 128 + 128)
                        wsl = slice(dc * D + ot * 512, dc * D + ot * 512 + 512)
                        last = te.matmul(ps, xfh_t[:, xsl], wvhi[:, wsl],
                                         start=(dc == 0), stop=False)
                        last = te.matmul(ps, xfl_t[:, xsl], wvhi[:, wsl],
                                         start=False, stop=False)
                        last = te.matmul(ps, xfh_t[:, xsl], wvlo[:, wsl],
                                         start=False, stop=(dc == NDC - 1))
                    return last
                P.step("T", vproj, reads=[nxh, nxl, "wv_hi", "wv_lo"],
                       writes=[psn], tag=f"p1V {tau}.{tc}.{ot}")
                hsl = slice(ot * 8, (ot + 1) * 8)
                vb = tc % 2
                vv3 = vst[vb].rearrange("p (h c) -> p h c", h=H)
                vhv = vv3[:, hsl, 0:64]
                vlv = vv3[:, hsl, 65:129]
                psv = ps.rearrange("p (h c) -> p h c", h=8)
                P.step("SC", lambda s, a=vhv, b=psv: s.activation(a, b, AF.Copy),
                       reads=[psn], writes=[f"vst{vb}"],
                       tag=f"p1V ev {tau}.{tc}.{ot}")
                P.step("V", lambda v, a=vlv, b=psv, c=vhv:
                       v.scalar_tensor_tensor(a, b, 1.0, c, ALU.mult, ALU.subtract),
                       reads=[psn, f"vst{vb}"], writes=[f"vst{vb}"],
                       tag=f"p1V lo {tau}.{tc}.{ot}")
            rows = slice(tau * SW + tc * 128, tau * SW + (tc + 1) * 128)
            vb = tc % 2
            dst = V_d[:, rows, :].transpose([1, 0, 2])
            dma("SY", dst, vst[vb].rearrange("p (h c) -> p h c", h=H),
                [f"vst{vb}"], [f"V_d#{tau}.{tc}"], f"p1 vd {tau}.{tc}")

    P.barrier()

    ln_split_transpose(x_own, "x_own", 0, "p15.0", buf=0)
    for j in range(NSTR):
        srow = j * SW
        qcols = slice(srow, srow + SW)
        buf = j % 2
        for oc in range(NDC):
            k_proj_part(oc, wkhi, wklo, "wk_hi", "wk_lo", f"p15K.{j}", buf=buf)
            if oc > 0:
                k_log_part(oc - 1, f"p15K.{j}")
            if j + 1 < NSTR:
                if oc % 2 == 1 and oc >= 3 and (oc - 2) // 2 < NTC:
                    ln_transpose_part(f"p15.{j + 1}", (oc - 2) // 2,
                                      (j + 1) % 2)
                if oc % 2 == 1 and oc // 2 < NTC:
                    ln_stats_part(x_own, "x_own", (j + 1) * SW,
                                  f"p15.{j + 1}", oc // 2)
        k_log_part(NDC - 1, f"p15K.{j}")
        if j + 1 < NSTR:
            for tcr in range(max(0, min(NTC, (NDC - 2) // 2)), NTC):
                ln_transpose_part(f"p15.{j + 1}", tcr, (j + 1) % 2)
        u_compute(u8own, "u8own", f"p15U.{j}")
        for hh in range(H):
            dma("GD", QA_d[hh, 64:80, qcols], u8own[:, 0:SW], ["u8own"],
                [f"QA_d#{j}.{hh}.u"], f"p15 ua {j}.{hh}")
        for oc in range(NDC):
            slot = oc % 2
            st32 = AR5[:, slot * 1024:slot * 1024 + 1024]
            st32v = st32.rearrange("p (dc o) -> p dc o", dc=NDC)
            osl = slice(oc * 128, (oc + 1) * 128)
            srcw = wq_d.rearrange("(dc p) o -> p dc o", p=128)[:, :, osl]
            nm32 = f"wq32_{slot}"
            qhi = AR6[:, slot * 1024:slot * 1024 + 512].bitcast(BF16)
            qlo = AR6[:, slot * 1024 + 512:slot * 1024 + 1024].bitcast(BF16)
            qhiv = qhi.rearrange("p (dc o) -> p dc o", dc=NDC)
            qlov = qlo.rearrange("p (dc o) -> p dc o", dc=NDC)
            nmh, nml = f"wqhi_{slot}", f"wqlo_{slot}"
            dma("SY", st32v, srcw, ["i", nm32], [nm32], f"p15 wq {j}.{oc}")
            P.step("SC", lambda s, a=qhi, b=st32: s.activation(
                a, b, AF.Copy, scale=0.125), reads=[nm32, nmh], writes=[nmh],
                tag=f"p15 wqh {j}.{oc}")
            P.step("V", lambda v, a=qlo, b=st32, c=qhi: v.scalar_tensor_tensor(
                a, b, 0.125, c, ALU.mult, ALU.subtract),
                reads=[nm32, nmh, nml], writes=[nml], tag=f"p15 wql {j}.{oc}")
            b = oc % 2
            ps, psn = (psA, "psA") if b == 0 else (psB, "psB")

            xfh_t, xfl_t, nxh, nxl = XF[buf]

            def qproj(te, qhiv=qhiv, qlov=qlov, ps=ps, xfh_t=xfh_t, xfl_t=xfl_t):
                last = None
                for dc in range(NDC):
                    tsl = slice(dc * SW, (dc + 1) * SW)
                    last = te.matmul(ps[:, 0:SW], qhiv[:, dc, :], xfh_t[:, tsl],
                                     start=(dc == 0), stop=False)
                    last = te.matmul(ps[:, 0:SW], qhiv[:, dc, :], xfl_t[:, tsl],
                                     start=False, stop=False)
                    last = te.matmul(ps[:, 0:SW], qlov[:, dc, :], xfh_t[:, tsl],
                                     start=False, stop=(dc == NDC - 1))
                return last
            P.step("T", qproj, reads=[nmh, nml, nxh, nxl], writes=[psn],
                   tag=f"p15 Q {j}.{oc}")
            P.step("SC", lambda s, ps=ps, b=b: s.activation(
                kh_st[b], ps[:, 0:SW], AF.Copy),
                reads=[psn], writes=[f"kh_st{b}"], tag=f"p15 qe {j}.{oc}")
            P.step("V", lambda v, ps=ps, b=b: v.scalar_tensor_tensor(
                kl_st[b], ps[:, 0:SW], 1.0, kh_st[b], ALU.mult, ALU.subtract),
                reads=[psn, f"kh_st{b}"], writes=[f"kl_st{b}"],
                tag=f"p15 ql {j}.{oc}")
            for hh, base in ((2 * oc, 0), (2 * oc + 1, 64)):
                sl = slice(base, base + 64)
                dma("SY", QA_d[hh, 0:64, qcols], kh_st[b][sl, 0:SW],
                    [f"kh_st{b}"], [f"QA_d#{j}.{hh}.a"], f"p15 qa {j}.{hh}")
                dma("SY", QA_d[hh, 80:144, qcols], kl_st[b][sl, 0:SW],
                    [f"kl_st{b}"], [f"QA_d#{j}.{hh}.b"], f"p15 qb {j}.{hh}")

    P.barrier()

    OHt, OLt = AR3, AR4
    KG = 8
    KAb = [SB(f"KAb{i}", [80, KG * 128], BF16) for i in range(2)]
    KBb = [SB(f"KBb{i}", [128, KG * 128], BF16) for i in range(2)]
    Vb = [SB(f"Vb{i}", [128, KG * 130], BF16) for i in range(2)]
    QAb = [SB(f"QAb{i}", [80, SW], BF16) for i in range(2)]
    QBb = [SB(f"QBb{i}", [128, SW], BF16) for i in range(2)]
    E32 = [SB(f"E32_{i}", [128, 512]) for i in range(2)]
    PE32 = [AR2[:, i * 1024:i * 1024 + 1024].bitcast(F32) for i in range(4)]
    Phi = [AR2[:, 4096 + i * 512:4096 + i * 512 + 512] for i in range(4)]
    Plo = [AR2[:, 6144 + i * 512:6144 + i * 512 + 512] for i in range(4)]
    O32 = AR5[0:64, 0:SW]
    RB32 = AR5[0:64, 512:512 + SW]
    Oh64 = AR5[0:64, 1024:1024 + SW // 2].bitcast(BF16)
    Ol64 = AR5[0:64, 1536:1536 + SW // 2].bitcast(BF16)
    rl32 = AR5[0:1, 3072:3072 + SW]
    rlhi = AR5[0:1, 3584:3584 + SW // 2].bitcast(BF16)
    rllo = AR5[0:1, 3584 + SW // 2:3584 + SW].bitcast(BF16)
    QP = SB("QP", [128, SW])
    CBIAS = AR1[:, 0:8 * SW]
    wo_hi = AR3[:, 4096:4096 + NDC * 512]
    wo_lo = AR4[:, 4096:4096 + NDC * 512]
    xres = AR7
    x1st = AR5[:, 2048:3072]

    for j in range(NSTR):
        R = cfg.rng_chunks(j)
        srow = j * SW

        def qpb(te, srow=srow):
            te.matmul(psD[:, 0:SW], ones1[:, 0:128],
                      qpos_hi[0:1, srow:srow + SW], start=True, stop=False)
            return te.matmul(psD[:, 0:SW], ones1[:, 0:128],
                             qpos_lo[0:1, srow:srow + SW],
                             start=False, stop=True)
        P.step("T", qpb, reads=["qpos_hi", "qpos_lo", "ones1", "psD"],
               writes=["psD"], tag=f"p2 qp {j}")
        P.step("SC", lambda s: s.activation(QP[:, 0:SW], psD[:, 0:SW], AF.Copy),
               reads=["psD"], writes=["QP"], tag=f"p2 qpe {j}")
        for e in range(8):
            kc = R - 8 + e
            if kc < 0:
                continue

            def cb(v, kc=kc, e=e):
                v.tensor_scalar(kcc, kcol, float(kc * 128), None, ALU.add)
                v.drain()
                return v.tensor_scalar(CBIAS[:, e * SW:(e + 1) * SW],
                                       QP[:, 0:SW], kcc[:, 0:1], -CBIG,
                                       ALU.is_lt, ALU.mult)
            P.step("V", cb, reads=["QP", "kcol"], writes=["CBIAS", "kcc"],
                   tag=f"p2 cb {j}.{e}")

        if j == DBGJ:
            dma("GD", dbgC[:, 6 * SW:8 * SW], CBIAS[:, 6 * SW:8 * SW], ["CBIAS"], ["dbgC"], "dbg cb")
            dma("GD", dbgP[:, 0:SW], QP[:, 0:SW], ["QP"], ["dbgP"], "dbg qp")
            P.step("GD", lambda raw: raw.dma_start(out=dbgP[:, SW:SW + 1], in_=kcc, allow_slow_non_contiguous=True), reads=["kcc"], writes=["dbgP"], tag="dbg kcc")
        pend = []
        for hh in range(H):
            qbuf = hh % 2
            qa, qb = QAb[qbuf], QBb[qbuf]
            nqa, nqb = f"QAb{qbuf}", f"QBb{qbuf}"
            qcols = slice(srow, srow + SW)
            dma("SY", qa, QA_d[hh, 0:80, qcols], ["QA_d", nqa], [nqa],
                f"p2 lqa {j}.{hh}")
            dma("SY", qb[0:64, :], QA_d[hh, 0:64, qcols], ["QA_d", nqb],
                [nqb + "#a"], f"p2 lqb {j}.{hh}")
            dma("SY", qb[64:128, :], QA_d[hh, 80:144, qcols], ["QA_d", nqb],
                [nqb + "#b"], f"p2 lqb2 {j}.{hh}")
            ngrp = (R + KG - 1) // KG
            psS = [(psA, "psA"), (psB, "psB"), (psC, "psC"), (psD, "psD")]
            LAG = 3

            def emit_qk(kc):
                g = kc // KG
                buf = g % 2
                ka, kb, vv = KAb[buf], KBb[buf], Vb[buf]
                nka, nkb, nv = f"KAb{buf}", f"KBb{buf}", f"Vb{buf}"
                if kc % KG == 0:
                    g0, g1 = g * KG * 128, min(R, (g + 1) * KG) * 128
                    gn = (g1 - g0) // 128
                    dma("SY", ka[:, 0:g1 - g0], KA_d[hh, 0:80, g0:g1],
                        ["KA_d", nka], [nka], f"p2 lka {j}.{hh}.{g}")
                    dma("SY", kb[0:64, 0:g1 - g0], KA_d[hh, 80:144, g0:g1],
                        ["KA_d", nkb], [nkb + "#a"], f"p2 lkb {j}.{hh}.{g}")
                    dma("SY", kb[64:128, 0:g1 - g0], KA_d[hh, 0:64, g0:g1],
                        ["KA_d", nkb], [nkb + "#b"], f"p2 lkb2 {j}.{hh}.{g}")
                    dma("SY", vv.rearrange("p (k c) -> p k c", c=130)[:, 0:gn, :],
                        V_d[hh, g0:g1, :].rearrange("(k p) c -> p k c", p=128),
                        ["V_d", nv], [nv], f"p2 lv {j}.{hh}.{g}")
                pb = kc % 4
                ps, psn = psS[pb]
                ksl = slice((kc % KG) * 128, (kc % KG) * 128 + 128)

                e = kc - (R - 8)

                def qk(te, ka=ka, kb=kb, ksl=ksl, ps=ps, qa=qa, qb=qb, e=e):
                    te.matmul(ps[:, 0:SW], ka[:, ksl], qa[0:80, :],
                              start=True, stop=False)
                    if e >= 0:
                        te.matmul(ps[:, 0:SW], kb[:, ksl], qb,
                                  start=False, stop=False)
                        return te.matmul(ps[:, 0:SW], identbf,
                                         CBIAS[:, e * SW:(e + 1) * SW],
                                         start=False, stop=True)
                    return te.matmul(ps[:, 0:SW], kb[:, ksl], qb,
                                     start=False, stop=True)
                P.step("T", qk, reads=[nka, nkb + "#a", nkb + "#b", nqa, nqb,
                                       "identbf", "CBIAS"],
                       writes=[psn], tag=f"p2 qk {j}.{hh}.{kc}")
                P.step("SC", lambda s, ps=ps, pb=pb: s.activation(
                    PE32[pb][:, 0:SW], ps[:, 0:SW], AF.Exp,
                    bias=nalpha[:, 0:1]),
                    reads=[psn, "nalpha"], writes=[f"PE32_{pb}"],
                    tag=f"p2 exf {j}.{hh}.{kc}")
                P.step("V", lambda v, pb=pb: v.tensor_copy(
                    Phi[pb][:, 0:SW].bitcast(BF16), PE32[pb][:, 0:SW]),
                    reads=[f"PE32_{pb}"], writes=[f"Phi{pb}"],
                    tag=f"p2 exh {j}.{hh}.{kc}")
                P.step("V", lambda v, pb=pb: v.scalar_tensor_tensor(
                    Plo[pb][:, 0:SW].bitcast(BF16), PE32[pb][:, 0:SW], 1.0,
                    Phi[pb][:, 0:SW].bitcast(BF16), ALU.mult, ALU.subtract),
                    reads=[f"PE32_{pb}", f"Phi{pb}"], writes=[f"Plo{pb}"],
                    tag=f"p2 pl {j}.{hh}.{kc}")

            def emit_pv(kc):
                g = kc // KG
                buf = g % 2
                vv = Vb[buf]
                nv = f"Vb{buf}"
                pb = kc % 4
                kg = kc % KG

                def pva(te, vv=vv, kg=kg, pb=pb, first=(kc == 0)):
                    vva = vv.rearrange("p (k c) -> p k c", c=130)
                    te.matmul(psO[0:65, 0:SW], vva[:, kg, 0:65],
                              Phi[pb][:, 0:SW].bitcast(BF16),
                              start=first, stop=False)
                    return te.matmul(psO[0:65, 0:SW], vva[:, kg, 65:130],
                                     Phi[pb][:, 0:SW].bitcast(BF16),
                                     start=False, stop=False)
                P.step("T", pva, reads=[nv, f"Phi{pb}", "psO"], writes=["psO"],
                       tag=f"p2 pva {j}.{hh}.{kc}")

                def pvb(te, vv=vv, kg=kg, pb=pb, last=(kc == R - 1)):
                    vva = vv.rearrange("p (k c) -> p k c", c=130)
                    return te.matmul(psO[0:65, 0:SW], vva[:, kg, 0:65],
                                     Plo[pb][:, 0:SW].bitcast(BF16),
                                     start=False, stop=last)
                P.step("T", pvb, reads=[nv, f"Plo{pb}", "psO"], writes=["psO"],
                       tag=f"p2 pvb {j}.{hh}.{kc}")

            for it in range(R + LAG):
                if it < R:
                    emit_qk(it)
                if it >= LAG:
                    emit_pv(it - LAG)

            def rfin(v):
                v.tensor_copy(rlhi[:, 0:SW], psO[64:65, 0:SW])
                v.drain()
                return v.scalar_tensor_tensor(rllo[:, 0:SW], psO[64:65, 0:SW],
                                              1.0, rlhi[:, 0:SW],
                                              ALU.mult, ALU.subtract)
            P.step("V", rfin, reads=["psO"], writes=["rlhi", "rllo"],
                   tag=f"p2 rf {j}.{hh}")

            def rb(te):
                te.matmul(psD[0:64, 0:SW], ones1[:, 0:64], rlhi[0:1, 0:SW],
                          start=True, stop=False)
                return te.matmul(psD[0:64, 0:SW], ones1[:, 0:64],
                                 rllo[0:1, 0:SW], start=False, stop=True)
            P.step("T", rb, reads=["rlhi", "rllo", "ones1", "psD"],
                   writes=["psD"], tag=f"p2 rbb {j}.{hh}")
            pair, half = divmod(hh, 2)
            oh = OHt[:, pair * SW:(pair + 1) * SW]
            ol = OLt[:, pair * SW:(pair + 1) * SW]
            hsl = slice(half * 64, half * 64 + 64)
            P.step("V", lambda v: v.reciprocal(RB32[:, 0:SW], psD[0:64, 0:SW]),
                   reads=["psD"], writes=["RB32"], tag=f"p2 rbe {j}.{hh}")
            P.step("V", lambda v: v.tensor_tensor(
                O32[:, 0:SW], psO[0:64, 0:SW], RB32[:, 0:SW], ALU.mult),
                reads=["psO", "RB32"], writes=["O32"], tag=f"p2 on {j}.{hh}")
            P.step("SC", lambda s: s.activation(Oh64[:, 0:SW], O32[:, 0:SW],
                                                AF.Copy),
                   reads=["O32"], writes=["Oh64"], tag=f"p2 oh {j}.{hh}")
            P.step("V", lambda v: v.scalar_tensor_tensor(
                Ol64[:, 0:SW], O32[:, 0:SW], 1.0, Oh64[:, 0:SW], ALU.mult,
                ALU.subtract),
                reads=["O32", "Oh64"], writes=["Ol64"], tag=f"p2 ol {j}.{hh}")
            dma("GD", oh[hsl, :], Oh64[:, 0:SW], ["Oh64", f"OH#{hh}"],
                [f"OH#{hh}"], f"p2 ohd {j}.{hh}")
            dma("GD", ol[hsl, :], Ol64[:, 0:SW], ["Ol64", f"OL#{hh}"],
                [f"OL#{hh}"], f"p2 old {j}.{hh}")
            if j == DBGJ:
                dma("GD", dbgO[hh], O32[:, 0:SW], ["O32"], ["dbgO"],
                    f"dbg o {hh}")

        for tc in range(NTC):
            rows = slice(srow + tc * 128, srow + (tc + 1) * 128)
            dma("SY", xres[:, tc * D:(tc + 1) * D], x_own[rows, :],
                ["x_own"], ["xres"], f"p2 xr {j}.{tc}")
        for ot in range(NOT):
            for cc in range(NDC):
                slot = cc % 2
                st32 = AR6[:, 2048 + slot * 512:2048 + slot * 512 + 512]
                srcw = wo_d.rearrange("(cc p) o -> p cc o", p=128)[
                    :, cc, ot * 512:(ot + 1) * 512]
                nmw = f"wo32_{slot}"
                dma("SY", st32, srcw, ["i", nmw], [nmw], f"p2 wo {j}.{ot}.{cc}")
                whs = wo_hi[:, cc * 512:(cc + 1) * 512]
                wls = wo_lo[:, cc * 512:(cc + 1) * 512]
                P.step("SC", lambda s, a=whs, b=st32: s.activation(a, b, AF.Copy),
                       reads=[nmw, "wo_hi"], writes=["wo_hi"],
                       tag=f"p2 woh {j}.{ot}.{cc}")
                P.step("V", lambda v, a=wls, b=st32, c=whs:
                       v.scalar_tensor_tensor(a, b, 1.0, c, ALU.mult,
                                              ALU.subtract),
                       reads=[nmw, "wo_hi", "wo_lo"], writes=["wo_lo"],
                       tag=f"p2 wol {j}.{ot}.{cc}")
            woh = wo_hi.rearrange("p (cc o) -> p cc o", cc=NDC)
            wol = wo_lo.rearrange("p (cc o) -> p cc o", cc=NDC)
            for tc in range(NTC):
                ps, psn = (psC, "psC") if tc % 2 == 0 else (psD, "psD")

                def wog(te, tc=tc, ps=ps):
                    last = None
                    for cc in range(NDC):
                        tsl = slice(cc * SW + tc * 128, cc * SW + tc * 128 + 128)
                        last = te.matmul(ps, OHt[:, tsl], woh[:, cc, :],
                                         start=(cc == 0), stop=False)
                        last = te.matmul(ps, OLt[:, tsl], woh[:, cc, :],
                                         start=False, stop=False)
                        last = te.matmul(ps, OHt[:, tsl], wol[:, cc, :],
                                         start=False, stop=(cc == NDC - 1))
                    return last
                P.step("T", wog, reads=["OH", "OL", "wo_hi", "wo_lo", psn],
                       writes=[psn], tag=f"p2 wog {j}.{ot}.{tc}")
                P.step("V", lambda v, ps=ps, tc=tc, ot=ot: v.tensor_tensor(
                    x1st[:, ot * 512:(ot + 1) * 512], ps,
                    xres[:, tc * D + ot * 512:tc * D + ot * 512 + 512], ALU.add),
                    reads=[psn, "xres", "x1st"], writes=["x1st"],
                    tag=f"p2 res {j}.{ot}.{tc}")
                rows = slice(srow + tc * 128, srow + (tc + 1) * 128)
                dma("GD", x1_d[rows, ot * 512:(ot + 1) * 512],
                    x1st[:, ot * 512:(ot + 1) * 512], ["x1st"],
                    [f"x1_d#{j}.{ot}.{tc}"], f"p2 x1 {j}.{ot}.{tc}")

    P.barrier()

    W1h = AR1[:, 0:3072]
    W1l = AR2[:, 0:3072]
    W2h = AR3
    W2l = AR4
    hh_t = AR1[:, 3072:3072 + FB * SW]
    hl_t = AR2[:, 3072:3072 + FB * SW]
    hr32 = E32
    x1t = AR7
    g32 = SB("g32", [128, 8])
    eg = SB("eg", [128, NE_R])
    w4 = SB("w4", [128, 8])
    c4 = SB("c4", [128, NE_R])
    c4all = SB("c4all", [128, NTC * NE_R])
    c4h = SB("c4h", [128, NE_R], BF16)
    c4l = SB("c4l", [128, NE_R], BF16)
    c6h = SB("c6h", [6, SW], BF16)
    c6l = SB("c6l", [6, SW], BF16)
    oh1 = SB("oh1", [128, NE_R])
    oh2 = SB("oh2", [128, NE_R])

    for tile in range(NSTR):
        trow = tile * SW
        ln_split_transpose(x1_d, "x1_d", trow, f"p3.{tile}")
        for tc in range(NTC):
            rows = slice(trow + tc * 128, trow + (tc + 1) * 128)
            dma("SY", x1t[:, tc * D:(tc + 1) * D], x1_d[rows, :],
                ["x1_d", "x1t"], ["x1t"], f"p3 x1t {tile}.{tc}")

        P.step("V", lambda v: v.memset(c6h, 0.5), writes=["c6h"],
               tag=f"p3 c6s {tile}")
        P.step("V", lambda v: v.memset(c6l, 0.0), writes=["c6l"],
               tag=f"p3 c6z {tile}")
        P.step("V", lambda v: v.memset(g32[:, NE_R:8], -3.0e38),
               reads=[], writes=["g32"], tag=f"p3 gpad {tile}")
        P.step("V", lambda v: v.memset(w4[:, NE_R:8], -3.0e38),
               reads=[], writes=["w4"], tag=f"p3 wpad {tile}")
        for tc in range(NTC):
            gsl = slice(tc * NE_R, (tc + 1) * NE_R)

            def gmm(te, tc=tc, gsl=gsl):
                for dc in range(NDC):
                    tsl = slice(dc * SW + tc * 128, dc * SW + tc * 128 + 128)
                    gh = gw_hi[:, dc * NE_R:(dc + 1) * NE_R]
                    gl = gw_lo[:, dc * NE_R:(dc + 1) * NE_R]
                    te.matmul(psO[:, gsl], xfh[:, tsl], gh,
                              start=(dc == 0), stop=False)
                    te.matmul(psO[:, gsl], xfh[:, tsl], gl, start=False, stop=False)
                    te.matmul(psO[:, gsl], xfl[:, tsl], gh, start=False, stop=False)
                te.matmul(psO[:, gsl], ones1[:, 0:128], gb_hi[0:1, :],
                          start=False, stop=False)
                return te.matmul(psO[:, gsl], ones1[:, 0:128], gb_lo[0:1, :],
                                 start=False, stop=True)
            P.step("T", gmm, reads=["xfh", "xfl", "gw_hi", "gw_lo", "gb_hi",
                                    "gb_lo", "ones1", "psO"],
                   writes=["psO"], tag=f"p3 g {tile}.{tc}")
            P.step("V", lambda v, gsl=gsl: v.tensor_copy(g32[:, 0:NE_R],
                                                          psO[:, gsl]),
                   reads=["psO"], writes=["g32"], tag=f"p3 gc {tile}.{tc}")

            def gmax(v):
                v.max(m8, g32)
                v.drain()
                return v.tensor_scalar(gcol[:, 0:1], m8[:, 0:1], -1.0, None,
                                       ALU.mult)
            P.step("V", gmax, reads=["g32"], writes=["m8", "gcol"],
                   tag=f"p3 gm {tile}.{tc}")
            P.step("SC", lambda s: s.activation(eg, g32[:, 0:NE_R], AF.Exp,
                                                bias=gcol[:, 0:1],
                                                accum_out=gcol[:, 1:2]),
                   reads=["g32", "gcol"], writes=["eg", "gcol"],
                   tag=f"p3 ge {tile}.{tc}")

            def gnorm(v):
                v.reciprocal(mcol[:, 6:7], gcol[:, 1:2])
                v.drain()
                return v.tensor_scalar(w4[:, 0:NE_R], eg, mcol[:, 6:7], None,
                                       ALU.mult)
            P.step("V", gnorm, reads=["eg", "gcol"], writes=["w4", "mcol"],
                   tag=f"p3 gn {tile}.{tc}")

            def top2(v, tc=tc):
                v.max(m8, w4)
                v.drain()
                v.max_index(mi8, m8, w4)
                v.drain()
                v.tensor_copy(mif, mi8)
                v.drain()
                v.tensor_scalar(oh1, iotaF[:, 0:NE_R], mif[:, 0:1], None,
                                ALU.is_equal)
                v.tensor_scalar(oh2, iotaF[:, 0:NE_R], mif[:, 1:2], None,
                                ALU.is_equal)
                v.drain()
                v.tensor_tensor(oh1, oh1, oh2, ALU.add)
                v.drain()
                v.tensor_tensor(c4, w4[:, 0:NE_R], oh1, ALU.mult)
                v.drain()
                return v.tensor_copy(c4all[:, tc * NE_R:(tc + 1) * NE_R], c4)
            P.step("V", top2, reads=["w4", "iotaF"],
                   writes=["c4", "c4all", "m8", "mi8", "mif", "oh1", "oh2"],
                   tag=f"p3 t2 {tile}.{tc}")
            if DBG and tile == 0 and tc == 0:
                dma("GD", dbgG[:, 0:4], g32[:, 0:4], ["g32"], ["dbgG"], "dbg g1")
                dma("GD", dbgG[:, 4:8], w4[:, 0:4], ["w4"], ["dbgG"], "dbg g2")
                dma("GD", dbgG[:, 8:12], c4, ["c4"], ["dbgG"], "dbg g3")
                dma("GD", dbgG[:, 12:14], mif[:, 0:2], ["mif"], ["dbgG"], "dbg g4")
            P.step("V", lambda v: v.tensor_copy(c4h, c4), reads=["c4"],
                   writes=["c4h"], tag=f"p3 ch {tile}.{tc}")
            P.step("V", lambda v: v.scalar_tensor_tensor(
                c4l, c4, 1.0, c4h, ALU.mult, ALU.subtract),
                reads=["c4", "c4h"], writes=["c4l"], tag=f"p3 cl {tile}.{tc}")
            P.step("T", lambda te: te.transpose(psT[0:NE_R, 0:128], c4h, identbf),
                   reads=["c4h", "identbf", "psT"], writes=["psT"],
                   tag=f"p3 ct {tile}.{tc}")
            P.step("V", lambda v, tc=tc: v.tensor_copy(
                c6h[0:NE_R, tc * 128:(tc + 1) * 128], psT[0:NE_R, 0:128]),
                reads=["psT"], writes=["c6h"], tag=f"p3 ce {tile}.{tc}")
            P.step("T", lambda te: te.transpose(psT[0:NE_R, 0:128], c4l, identbf),
                   reads=["c4l", "identbf", "psT"], writes=["psT"],
                   tag=f"p3 ct2 {tile}.{tc}")
            P.step("V", lambda v, tc=tc: v.tensor_copy(
                c6l[0:NE_R, tc * 128:(tc + 1) * 128], psT[0:NE_R, 0:128]),
                reads=["psT"], writes=["c6l"], tag=f"p3 ce2 {tile}.{tc}")


        for ex in (DBG_EXPERTS if DBG_EXPERTS else range(NEX)):
            shared = ex < NE_S
            w1d = sw1_d if shared else rw1_d
            w2d = sw2_d if shared else rw2_d
            ei = ex if shared else ex - NE_S
            for fb in range(NFB):
                def emit_w(fi):
                    fc = fb * FB + fi
                    fsl = slice(fc * 128, (fc + 1) * 128)
                    slot = fi % 3
                    w1s32 = AR5[:, slot * 1024:slot * 1024 + 1024]
                    w1v = w1s32.rearrange("p (dc f) -> p dc f", dc=NDC)
                    src1 = w1d[ei].rearrange("(dc p) f -> p dc f", p=128)[:, :, fsl]
                    h1 = W1h[:, slot * 1024:slot * 1024 + 1024]
                    l1 = W1l[:, slot * 1024:slot * 1024 + 1024]
                    nm1 = f"w1s{slot}"
                    dma("SY", w1v, src1, ["i", nm1], [nm1],
                        f"p3 w1 {tile}.{ex}.{fc}")
                    P.step("SC", lambda s, a=h1, b=w1s32: s.activation(
                        a, b, AF.Copy), reads=[nm1, nm1 + "h"],
                        writes=[nm1 + "h"], tag=f"p3 w1h {tile}.{ex}.{fc}")
                    P.step("V", lambda v, a=l1, b=w1s32, c=h1:
                           v.scalar_tensor_tensor(a, b, 1.0, c, ALU.mult,
                                                  ALU.subtract),
                           reads=[nm1, nm1 + "h", nm1 + "l"], writes=[nm1 + "l"],
                           tag=f"p3 w1l {tile}.{ex}.{fc}")
                    w2s32 = AR6[:, slot * 1024:slot * 1024 + 1024]
                    src2 = w2d[ei][fsl, :]
                    h2 = W2h[:, fi * 1024:(fi + 1) * 1024]
                    l2 = W2l[:, fi * 1024:(fi + 1) * 1024]
                    nm2 = f"w2s{slot}"
                    dma("SY", w2s32, src2, ["i", nm2], [nm2],
                        f"p3 w2 {tile}.{ex}.{fc}")
                    P.step("SC", lambda s, a=h2, b=w2s32: s.activation(
                        a, b, AF.Copy), reads=[nm2, "W2h"], writes=["W2h"],
                        tag=f"p3 w2h {tile}.{ex}.{fc}")
                    P.step("V", lambda v, a=l2, b=w2s32, c=h2:
                           v.scalar_tensor_tensor(a, b, 1.0, c, ALU.mult,
                                                  ALU.subtract),
                           reads=[nm2, "W2h", "W2l"], writes=["W2l"],
                           tag=f"p3 w2l {tile}.{ex}.{fc}")

                def emit_h(fi):
                    fc = fb * FB + fi
                    slot = fi % 3
                    sp = fi % 2
                    h1 = W1h[:, slot * 1024:slot * 1024 + 1024]
                    l1 = W1l[:, slot * 1024:slot * 1024 + 1024]
                    h1v = h1.rearrange("p (dc f) -> p dc f", dc=NDC)
                    l1v = l1.rearrange("p (dc f) -> p dc f", dc=NDC)
                    nm1 = f"w1s{slot}"
                    ps, psn = (psA, "psA") if sp == 0 else (psB, "psB")

                    def hmm(te, h1v=h1v, l1v=l1v, ps=ps):
                        last = None
                        for dc in range(NDC):
                            tsl = slice(dc * SW, (dc + 1) * SW)
                            last = te.matmul(ps[:, 0:SW], h1v[:, dc, :],
                                             xfh[:, tsl],
                                             start=(dc == 0), stop=False)
                            last = te.matmul(ps[:, 0:SW], h1v[:, dc, :],
                                             xfl[:, tsl],
                                             start=False, stop=False)
                            last = te.matmul(ps[:, 0:SW], l1v[:, dc, :],
                                             xfh[:, tsl],
                                             start=False, stop=(dc == NDC - 1))
                        return last
                    P.step("T", hmm, reads=[nm1 + "h", nm1 + "l", "xfh", "xfl"],
                           writes=[psn], tag=f"p3 h {tile}.{ex}.{fc}")
                    bcol = b1cols[:, ex * NFC + fc:ex * NFC + fc + 1]
                    P.step("SC", lambda s, ps=ps, bcol=bcol, slot=sp,
                           sh=shared: s.activation(
                               hr32[slot][:, 0:SW], ps[:, 0:SW], AF.Relu,
                               bias=bcol, scale=(0.5 if sh else 1.0)),
                           reads=[psn, "b1cols"], writes=[f"E32_{sp}"],
                           tag=f"p3 hr {tile}.{ex}.{fc}")
                    hhs = hh_t[:, fi * SW:(fi + 1) * SW]
                    hls = hl_t[:, fi * SW:(fi + 1) * SW]
                    P.step("V", lambda v, a=hhs, slot=sp: v.tensor_copy(
                        a, hr32[slot][:, 0:SW]),
                        reads=[f"E32_{sp}", "hh_t"], writes=["hh_t"],
                        tag=f"p3 hh {tile}.{ex}.{fc}")
                    P.step("V", lambda v, a=hls, b=hhs, slot=sp:
                           v.scalar_tensor_tensor(a, hr32[slot][:, 0:SW], 1.0, b,
                                                  ALU.mult, ALU.subtract),
                           reads=[f"E32_{sp}", "hh_t", "hl_t"],
                           writes=["hl_t"], tag=f"p3 hl {tile}.{ex}.{fc}")

                for it in range(FB + 2):
                    if it < FB:
                        emit_w(it)
                    if it >= 2:
                        emit_h(it - 2)
                for tc in range(NTC):
                    for ot in range(NOT):
                        i = tc * NOT + ot
                        ps, psn = (psC, "psC") if i % 2 == 0 else (psD, "psD")

                        def ymm(te, tc=tc, ot=ot, ps=ps):
                            last = None
                            for fi in range(FB):
                                hsl = slice(fi * SW + tc * 128,
                                            fi * SW + tc * 128 + 128)
                                w2hs = W2h[:, fi * 1024 + ot * 512:
                                           fi * 1024 + ot * 512 + 512]
                                w2ls = W2l[:, fi * 1024 + ot * 512:
                                           fi * 1024 + ot * 512 + 512]
                                last = te.matmul(ps, hh_t[:, hsl], w2hs,
                                                 start=(fi == 0), stop=False)
                                last = te.matmul(ps, hl_t[:, hsl], w2hs,
                                                 start=False, stop=False)
                                last = te.matmul(ps, hh_t[:, hsl], w2ls,
                                                 start=False, stop=(fi == FB - 1))
                            return last
                        P.step("T", ymm, reads=["hh_t", "hl_t", "W2h", "W2l", psn],
                               writes=[psn], tag=f"p3 y {tile}.{ex}.{fb}.{i}")
                        if DBG and tile == 0 and ex == DBG_YEX and fb == 0 and i == 0:
                            P.step("SC", lambda s, ps=ps: s.activation(
                                dbgYs, ps, AF.Copy), reads=[psn],
                                writes=["dbgYs"], tag="dbg yev")
                            dma("GD", dbgY, dbgYs, ["dbgYs"], ["dbgY"], "dbg y")
                        xsl = slice(tc * D + ot * 512, tc * D + ot * 512 + 512)
                        if shared:
                            P.step("V", lambda v, ps=ps, xsl=xsl: v.tensor_tensor(
                                x1t[:, xsl], ps, x1t[:, xsl], ALU.add),
                                reads=[psn, "x1t"], writes=["x1t"],
                                tag=f"p3 ya {tile}.{ex}.{fb}.{i}")
                        else:
                            ccol = c4all[:, tc * NE_R + ei:tc * NE_R + ei + 1]
                            P.step("V", lambda v, ps=ps, xsl=xsl, ccol=ccol:
                                   v.scalar_tensor_tensor(
                                       x1t[:, xsl], ps, ccol, x1t[:, xsl],
                                       ALU.mult, ALU.add),
                                   reads=[psn, "x1t", "c4all"], writes=["x1t"],
                                   tag=f"p3 ya {tile}.{ex}.{fb}.{i}")

        for tc in range(NTC):
            for ot in range(NOT):
                i = tc * NOT + ot
                ps, psn = (psC, "psC") if i % 2 == 0 else (psD, "psD")

                def b2mm(te, tc=tc, ot=ot, ps=ps):
                    csl = slice(tc * 128, (tc + 1) * 128)
                    osl = slice(ot * 512, (ot + 1) * 512)
                    te.matmul(ps, c6h[:, csl], b2_hi[:, osl], start=True,
                              stop=False)
                    te.matmul(ps, c6l[:, csl], b2_hi[:, osl], start=False,
                              stop=False)
                    return te.matmul(ps, c6h[:, csl], b2_lo[:, osl],
                                     start=False, stop=True)
                P.step("T", b2mm, reads=["c6h", "c6l", "b2_hi", "b2_lo", psn],
                       writes=[psn], tag=f"p3 b2 {tile}.{i}")
                xsl = slice(tc * D + ot * 512, tc * D + ot * 512 + 512)
                P.step("V", lambda v, ps=ps, xsl=xsl: v.tensor_tensor(
                    x1t[:, xsl], ps, x1t[:, xsl], ALU.add),
                    reads=[psn, "x1t"], writes=["x1t"], tag=f"p3 b2a {tile}.{i}")

        for tc in range(NTC):
            rows = slice(trow + tc * 128, trow + (tc + 1) * 128)
            dma("GD", y_d[rows, :], x1t[:, tc * D:(tc + 1) * D],
                ["x1t"], [f"y_dram#{tile}.{tc}"], f"p3 out {tile}.{tc}")

    P.step("G", lambda g: g.memset(dummy, 0.0), reads=["y_dram"],
           writes=["dummy"], tag="final join")

    with nc.Block() as block:
        P.emit(block)
    return nc


_CACHE = {}


def _get_program(cfg_key):
    if cfg_key not in _CACHE:
        _CACHE[cfg_key] = build_program(Cfg())
    return _CACHE[cfg_key]


def kernel(L0_attn_x, L0_attn_wq, L0_attn_wk, L0_attn_wv, L0_attn_wo,
           L0_attn_wh, L0_ffn_gate_w, L0_ffn_gate_b, L0_ffn_sw1, L0_ffn_sb1,
           L0_ffn_sw2, L0_ffn_sb2, L0_ffn_rw1, L0_ffn_rb1, L0_ffn_rw2,
           L0_ffn_rb2):
    cfg = Cfg()
    x = np.ascontiguousarray(np.asarray(L0_attn_x, dtype=np.float32))
    Bn, TF, _ = x.shape
    n_cores = 8
    nc = _get_program("full")

    common = {
        "wq": np.ascontiguousarray(np.asarray(L0_attn_wq, np.float32)),
        "wk": np.ascontiguousarray(np.asarray(L0_attn_wk, np.float32)),
        "wv": np.ascontiguousarray(np.asarray(L0_attn_wv, np.float32)),
        "wo": np.ascontiguousarray(np.asarray(L0_attn_wo, np.float32)),
        "wh": np.ascontiguousarray(np.asarray(L0_attn_wh, np.float32)),
        "gate_w": np.ascontiguousarray(np.asarray(L0_ffn_gate_w, np.float32)),
        "gate_b": np.ascontiguousarray(np.asarray(L0_ffn_gate_b, np.float32)).reshape(1, NE_R),
        "sw1": np.ascontiguousarray(np.asarray(L0_ffn_sw1, np.float32)),
        "sb1": np.ascontiguousarray(np.asarray(L0_ffn_sb1, np.float32)),
        "sw2": np.ascontiguousarray(np.asarray(L0_ffn_sw2, np.float32)),
        "sb2": np.ascontiguousarray(np.asarray(L0_ffn_sb2, np.float32)),
        "rw1": np.ascontiguousarray(np.asarray(L0_ffn_rw1, np.float32)),
        "rb1": np.ascontiguousarray(np.asarray(L0_ffn_rb1, np.float32)),
        "rw2": np.ascontiguousarray(np.asarray(L0_ffn_rw2, np.float32)),
        "rb2": np.ascontiguousarray(np.asarray(L0_ffn_rb2, np.float32)),
    }
    in_maps = []
    rows_per_core = []
    for c in range(n_cores):
        b, p = divmod(c, 2)
        stripe_ids = [2 * jj + p for jj in range(cfg.NSTR)]
        own = np.concatenate([np.arange(s * cfg.SW, (s + 1) * cfg.SW)
                              for s in stripe_ids])
        rows_per_core.append((b, own))
        m = dict(common)
        m["x_full"] = x[b]
        m["x_own"] = np.ascontiguousarray(x[b][own])
        m["qpos"] = np.ascontiguousarray(own.astype(np.float32)).reshape(1, -1)
        in_maps.append(m)

    res = run_bass_kernel_spmd(nc, in_maps, core_ids=list(range(n_cores)))
    out = np.zeros_like(x)
    for c in range(n_cores):
        b, own = rows_per_core[c]
        out[b][own] = np.asarray(res.results[c]["y"])
    return out


# version 20
# speedup vs baseline: 1.1626x; 1.1626x over previous
import numpy as np
import concourse.bass as bass
import concourse.mybir as mybir
from concourse.bass_utils import run_bass_kernel_spmd

F32 = mybir.dt.float32
BF16 = mybir.dt.bfloat16
U32 = mybir.dt.uint32
AF = mybir.ActivationFunctionType
ALU = mybir.AluOpType

D, H, HD, NB = 1024, 16, 64, 16
NE_R, NE_S = 4, 2
NEX = NE_S + NE_R
NDC = D // 128
ALPHA = 64.0
CBIG = 65536.0
EPS = 1e-5
DBG = False
X1OUT = False
DBGJ = -1
DBG_EXPERTS = []
DBG_YEX = -1

LANES = ("T", "V", "SC", "SD", "G", "GD", "SY")
_SECTION = {"T": "tensor", "V": "vector", "SC": "scalar", "SD": "scalar",
            "G": "gpsimd", "GD": "gpsimd", "SY": "sync"}
_DMA_LANES = {"SD", "GD", "SY"}


class Cfg:
    def __init__(self, TF=4096, SW=512, NSTR=4, FF=4096, FB=8):
        self.TF, self.SW, self.NSTR, self.FF, self.FB = TF, SW, NSTR, FF, FB
        self.TO = SW * NSTR
        self.NTAU = TF // SW
        self.NTC = SW // 128
        self.NFC = FF // 128

    def rng_chunks(self, j):
        return min((2 * j + 2) * self.SW // 128, self.TF // 128)


class Plan:

    def __init__(self, nc):
        self.nc = nc
        self.steps = []
        self.cnt = {}
        self.sem = {}
        self.last_w = {}
        self.readers = {}

    DRAM_BASES = {"KA_d", "KB_d", "V_d", "QA_d", "QB_d", "x1_d", "y_dram",
                  "dbgQ", "dbgQB", "dbgO", "dbgC", "dbgP", "dbgG", "dbgCB",
                  "dbgH", "dbgW2", "dbgY"}

    def _semkey(self, lane, reads, writes):
        if lane in _DMA_LANES:
            tgt = writes[0] if writes else "misc"
            if tgt in self.DRAM_BASES and reads:
                tgt = reads[0]
            return "D" + lane + "_" + tgt
        return lane

    def _get_sem(self, key):
        if key not in self.sem:
            self.sem[key] = self.nc.alloc_semaphore("S_" + key)
            self.cnt[key] = 0
        return self.sem[key]

    @staticmethod
    def _parse(name):
        if "#" in name:
            b, s = name.split("#", 1)
            return b, s
        return name, "*"

    def step(self, lane, emit, reads=(), writes=(), tag=""):
        key = self._semkey(lane, [self._parse(r)[0] for r in reads],
                           [self._parse(w)[0] for w in writes])
        self._get_sem(key)
        waits = {}
        drain = [False]

        def need(dep):
            dl, dkey, dcnt = dep
            if dl == lane and lane not in _DMA_LANES and dkey == lane:
                if lane != "T":
                    drain[0] = True
            else:
                if dkey.startswith("D"):
                    dcnt = self.cnt[dkey]
                waits[dkey] = max(waits.get(dkey, 0), dcnt)

        def sufs_conflicting(d, s):
            if s == "*":
                return list(d.keys())
            out = [t for t in (s, "*") if t in d]
            return out

        for r in reads:
            b, s = self._parse(r)
            d = self.last_w.get(b, {})
            for t in sufs_conflicting(d, s):
                need(d[t])
        for wn in writes:
            b, s = self._parse(wn)
            d = self.last_w.get(b, {})
            for t in sufs_conflicting(d, s):
                need(d[t])
            rd = self.readers.get(b, {})
            for t in sufs_conflicting(rd, s):
                for (rl, rkey), rc in rd[t].items():
                    need((rl, rkey, rc))
        inc = 16 if lane in _DMA_LANES else 1
        self.cnt[key] += inc
        mc = self.cnt[key]
        for wn in writes:
            b, s = self._parse(wn)
            d = self.last_w.setdefault(b, {})
            if s == "*":
                d.clear()
            d[s] = (lane, key, mc)
            rd = self.readers.setdefault(b, {})
            if s == "*":
                rd.clear()
            else:
                rd.pop(s, None)
                rd.pop("*", None)
        for r in reads:
            b, s = self._parse(r)
            self.readers.setdefault(b, {}).setdefault(s, {})[(lane, key)] = mc
        self.steps.append(dict(lane=lane, key=key, emit=emit, waits=waits,
                               drain=drain[0], inc=inc, tag=tag))

    def barrier(self):
        self.steps.append(dict(lane=None, waits=dict(self.cnt), tag="barrier"))

    def emit(self, block):
        seen = {ln: {} for ln in LANES}
        pend = {ln: {} for ln in LANES}
        merged = {}
        for s in self.steps:
            if s["lane"] is None:
                for ln in LANES:
                    for k, c in s["waits"].items():
                        if k != ln and c > 0:
                            pend[ln][k] = max(pend[ln].get(k, 0), c)
                continue
            ln = s["lane"]
            waits = dict(pend[ln])
            pend[ln] = {}
            for k, c in s["waits"].items():
                waits[k] = max(waits.get(k, 0), c)
            eff = {}
            for k, c in waits.items():
                if k == ln:
                    continue
                if c > seen[ln].get(k, 0):
                    eff[k] = c
                    seen[ln][k] = c
            merged.setdefault(_SECTION[ln], []).append(
                (ln, s["key"], eff, s["drain"], s["emit"], s["inc"], s["tag"]))

        for sec, items in merged.items():
            def body(raw, items=items):
                for ln, key, eff, drain, emitfn, inc, tag in items:
                    for k, c in eff.items():
                        raw.wait_ge(self.sem[k], c)
                    if drain:
                        raw.drain()
                    last = emitfn(raw)
                    assert last is not None, f"step {tag!r} on {ln} returned None"
                    last.then_inc(self.sem[key], inc)
            getattr(block, sec)(body)


def build_program(cfg: Cfg):
    nc = bass.Bass("TRN2", target_bir_lowering=False)
    TF, SW, NSTR, FF, FB = cfg.TF, cfg.SW, cfg.NSTR, cfg.FF, cfg.FB
    TO, NTAU, NTC, NFC = cfg.TO, cfg.NTAU, cfg.NTC, cfg.NFC
    NOT = D // 512
    NFB = NFC // FB

    def I(name, shape, dt=F32):
        return nc.dram_tensor(name, shape, dt, kind="ExternalInput").ap()

    x_full = I("x_full", [TF, D])
    x_own = I("x_own", [TO, D])
    qpos_d = I("qpos", [1, TO])
    wq_d = I("wq", [D, D]); wk_d = I("wk", [D, D])
    wv_d = I("wv", [D, D]); wo_d = I("wo", [D, D])
    wh_d = I("wh", [D, NB])
    gw_d = I("gate_w", [D, NE_R]); gb_d = I("gate_b", [1, NE_R])
    sw1_d = I("sw1", [NE_S, D, FF]); sb1_d = I("sb1", [NE_S, FF])
    sw2_d = I("sw2", [NE_S, FF, D]); sb2_d = I("sb2", [NE_S, D])
    rw1_d = I("rw1", [NE_R, D, FF]); rb1_d = I("rb1", [NE_R, FF])
    rw2_d = I("rw2", [NE_R, FF, D]); rb2_d = I("rb2", [NE_R, D])
    y_d = nc.dram_tensor("y", [TO, D], F32, kind="ExternalOutput").ap()

    KA_d = nc.dram_tensor("KA_d", [H, 144, TF], BF16).ap()
    V_d = nc.dram_tensor("V_d", [H, TF, 130], BF16).ap()
    x1_d = nc.dram_tensor("x1_d", [TO, D], F32,
                          kind=("ExternalOutput" if X1OUT else "Internal")).ap()
    QA_d = nc.dram_tensor("QA_d", [H, 144, TO], BF16).ap()
    dbgQ = nc.dram_tensor("dbgQ", [128, H * SW], mybir.dt.bfloat16).ap()
    dbgQB = nc.dram_tensor("dbgQB", [128, H * SW], mybir.dt.bfloat16).ap()
    dbgO = nc.dram_tensor("dbgO", [H, 64, SW], F32).ap()
    dbgC = nc.dram_tensor("dbgC", [128, 8 * SW], F32).ap()
    dbgP = nc.dram_tensor("dbgP", [128, SW + 8], F32).ap()
    dbgG = nc.dram_tensor("dbgG", [128, 16], F32).ap()
    dbgCB = nc.dram_tensor("dbgCB", [4, 128, SW], F32).ap()
    dbgH = nc.dram_tensor("dbgH", [2, 128, SW], mybir.dt.bfloat16).ap()
    dbgW2 = nc.dram_tensor("dbgW2", [2, 128, 4096], mybir.dt.bfloat16).ap()
    dbgY = nc.dram_tensor("dbgY", [128, 512], F32).ap()
    dbgYs = nc.alloc_sbuf_tensor("dbgYs", [128, 512], F32).ap() if DBG else None

    P = Plan(nc)
    SB = lambda n, sh, dt=F32: nc.alloc_sbuf_tensor(n, sh, dt).ap()

    identbf = SB("identbf", [128, 128], BF16)
    iotaF = SB("iotaF", [128, 128])
    kcol = SB("kcol", [128, 1])
    kcc = SB("kcc", [128, 1])
    ones1 = SB("ones1", [1, 128], BF16)
    nalpha = SB("nalpha", [128, 1])
    qpos_hi = SB("qpos_hi", [1, TO], BF16)
    qpos_lo = SB("qpos_lo", [1, TO], BF16)
    wh_32 = SB("wh_32", [128, NDC * NB])
    wh_hi = SB("wh_hi", [128, NDC * NB], BF16)
    wh_lo = SB("wh_lo", [128, NDC * NB], BF16)
    gw_32 = SB("gw_32", [128, NDC * NE_R])
    gw_hi = SB("gw_hi", [128, NDC * NE_R], BF16)
    gw_lo = SB("gw_lo", [128, NDC * NE_R], BF16)
    gb_32 = SB("gb_32", [1, NE_R])
    gb_hi = SB("gb_hi", [1, NE_R], BF16)
    gb_lo = SB("gb_lo", [1, NE_R], BF16)
    b1r_32 = None
    b1r_hi = None
    b1r_lo = None
    b1cols = SB("b1cols", [128, NEX * NFC])

    b2_hi = SB("b2_hi", [NEX, D], BF16)
    b2_lo = SB("b2_lo", [NEX, D], BF16)
    dummy = SB("kdummy", [1, 8])

    AR1 = SB("AR1", [128, 8192], BF16)
    AR2 = SB("AR2", [128, 8192], BF16)
    AR3 = SB("AR3", [128, 8192], BF16)
    AR4 = SB("AR4", [128, 8192], BF16)
    AR5 = SB("AR5", [128, 4096])
    AR6 = SB("AR6", [128, 4096])
    AR7 = SB("AR7", [128, 4096])

    b1r_32 = AR5[0:NEX, 0:FF]
    b1r_hi = AR1[0:NEX, 0:FF]
    b1r_lo = AR2[0:NEX, 0:FF]
    qpos_sb = AR6[0:1, 0:TO]
    b2_32 = AR6[0:NEX, 2048:2048 + D]

    xfh = SB("xfh", [128, NDC * SW], BF16)
    xfl = SB("xfl", [128, NDC * SW], BF16)
    xfhB = AR7[:, 0:NDC * SW // 2].bitcast(BF16)
    xflB = AR7[:, NDC * SW // 2:NDC * SW].bitcast(BF16)
    XF = [(xfh, xfl, "xfh", "xfl"), (xfhB, xflB, "xfhB", "xflB")]
    xst = SB("xst", [128, D])
    xsq = SB("xsq", [128, D])
    xhh = [SB(f"xhh{i}", [128, D], BF16) for i in range(2)]
    xhl = [SB(f"xhl{i}", [128, D], BF16) for i in range(2)]
    mcol = SB("mcol", [128, 8])
    gcol = SB("gcol", [128, 2])

    NKST = 4
    kh_st = [SB(f"kh_st{i}", [128, SW], BF16) for i in range(NKST)]
    kl_st = [SB(f"kl_st{i}", [128, SW], BF16) for i in range(NKST)]
    vst = [SB(f"vst{i}", [128, H * 130], BF16) for i in range(2)]
    lg32 = SB("lg32", [128, NTC * NB])
    m8 = SB("m8", [128, 8])
    mi8 = SB("mi8", [128, 8], U32)
    mif = SB("mif", [128, 8])
    u8t = SB("u8t", [128, NB], BF16)
    u8f = SB("u8f", [16, SW], BF16)
    u8acc = SB("u8acc", [16, TO], BF16)
    pcol = SB("pcol", [16, 2])
    pbf = SB("pbf", [1, 1], BF16)

    psA = nc.alloc_psum_tensor("psA", [128, 512], F32).ap()
    psB = nc.alloc_psum_tensor("psB", [128, 512], F32).ap()
    psC = nc.alloc_psum_tensor("psC", [128, 512], F32).ap()
    psD = nc.alloc_psum_tensor("psD", [128, 512], F32).ap()
    psO = nc.alloc_psum_tensor("psO", [128, 512], F32).ap()
    psT = nc.alloc_psum_tensor("psT", [128, 1024], BF16).ap()

    def dma(lane, dst, src, reads, writes, tag=""):
        P.step(lane, lambda raw, d=dst, s=src: raw.dma_start(out=d, in_=s),
               reads=reads, writes=writes, tag=tag)

    dma("SY", qpos_sb, qpos_d, ["i"], ["qpos_sb"], "p0 qpos")
    dma("SY", gb_32, gb_d, ["i"], ["gb_32"], "p0 gb")
    dma("SY", wh_32.rearrange("p (dc n) -> p dc n", dc=NDC),
        wh_d.rearrange("(dc p) n -> p dc n", p=128), ["i"], ["wh_32"], "p0 wh")
    dma("SY", gw_32.rearrange("p (dc n) -> p dc n", dc=NDC),
        gw_d.rearrange("(dc p) n -> p dc n", p=128), ["i"], ["gw_32"], "p0 gw")
    dma("SY", b1r_32[0:NE_S, :], sb1_d, ["i"], ["b1r_32"], "p0 sb1")
    dma("SY", b1r_32[NE_S:NEX, :], rb1_d, ["i"], ["b1r_32"], "p0 rb1")
    dma("SY", b2_32[NE_R:NEX, :], sb2_d, ["i"], ["b2_32"], "p0 sb2")
    dma("SY", b2_32[0:NE_R, :], rb2_d, ["i"], ["b2_32"], "p0 rb2")

    P.step("G", lambda g: g.iota(iotaF, pattern=[[1, 128]], base=0,
                                 channel_multiplier=0,
                                 allow_small_or_imprecise_dtypes=True),
           writes=["iotaF"], tag="p0 iotaF")
    P.step("G", lambda g: g.iota(kcol, pattern=[[1, 1]], base=0,
                                 channel_multiplier=1,
                                 allow_small_or_imprecise_dtypes=True),
           writes=["kcol"], tag="p0 kcol")

    def mk_ident(v):
        v.tensor_scalar(xst[:, 0:128], iotaF, kcol[:, 0:1], None, ALU.subtract)
        v.drain()
        return v.tensor_scalar(identbf, xst[:, 0:128], 0.0, None, ALU.is_equal)
    P.step("V", mk_ident, reads=["iotaF", "kcol"], writes=["identbf", "xst"],
           tag="p0 ident")
    P.step("V", lambda v: v.memset(ones1, 1.0), writes=["ones1"], tag="p0 ones")
    P.step("V", lambda v: v.memset(nalpha, -ALPHA), writes=["nalpha"], tag="p0 na")
    P.step("V", lambda v: v.tensor_scalar(b1r_32[0:NE_S, :], b1r_32[0:NE_S, :],
                                          0.5, None, ALU.mult),
           reads=["b1r_32"], writes=["b1r_32"], tag="p0 b1half")

    for nm, w32, whi, wlo in (("wh", wh_32, wh_hi, wh_lo),
                              ("gw", gw_32, gw_hi, gw_lo),
                              ("gb", gb_32, gb_hi, gb_lo),
                              ("qpos", qpos_sb, qpos_hi, qpos_lo),
                              ("b1r", b1r_32, b1r_hi, b1r_lo),
                              ("b2", b2_32, b2_hi, b2_lo)):
        srcn = {"wh": "wh_32", "gw": "gw_32", "gb": "gb_32",
                "qpos": "qpos_sb", "b1r": "b1r_32", "b2": "b2_32"}[nm]
        P.step("V", lambda v, a=w32, b=whi: v.tensor_copy(b, a),
               reads=[srcn], writes=[nm + "_hi"], tag=f"p0 {nm} hi")
        P.step("V", lambda v, a=w32, b=whi, c=wlo:
               v.scalar_tensor_tensor(c, a, 1.0, b, ALU.mult, ALU.subtract),
               reads=[srcn, nm + "_hi"], writes=[nm + "_lo"], tag=f"p0 {nm} lo")

    b1cv = b1cols.rearrange("p (e f) -> p e f", e=NEX)
    for fc in range(NFC):
        fsl = slice(fc * 128, (fc + 1) * 128)
        P.step("T", lambda te, fsl=fsl: te.transpose(
            psT[0:128, 0:NEX], b1r_hi[:, fsl], identbf[0:NEX, 0:NEX]),
            reads=["b1r_hi", "identbf", "psT"], writes=["psT"],
            tag=f"p0 b1t {fc}")
        P.step("V", lambda v, fc=fc: v.tensor_copy(
            b1cv[:, :, fc], psT[0:128, 0:NEX]),
            reads=["psT"], writes=["b1cols"], tag=f"p0 b1e {fc}")
        P.step("T", lambda te, fsl=fsl: te.transpose(
            psT[0:128, 0:NEX], b1r_lo[:, fsl], identbf[0:NEX, 0:NEX]),
            reads=["b1r_lo", "identbf", "psT"], writes=["psT"],
            tag=f"p0 b1t2 {fc}")
        P.step("V", lambda v, fc=fc: v.tensor_tensor(
            b1cv[:, :, fc], b1cv[:, :, fc], psT[0:128, 0:NEX], ALU.add),
            reads=["psT", "b1cols"], writes=["b1cols"], tag=f"p0 b1a {fc}")

    P.step("V", lambda v: v.tensor_scalar(pbf, qpos_sb[0:1, 0:1], 1.0 / SW,
                                          None, ALU.mult),
           reads=["qpos_sb"], writes=["pbf"], tag="p0 par")
    P.step("T", lambda te: te.matmul(psO[0:16, 0:1], ones1[0:1, 0:16], pbf,
                                     start=True, stop=True),
           reads=["ones1", "pbf", "psO"], writes=["psO"], tag="p0 parb")
    P.step("V", lambda v: v.tensor_copy(pcol[:, 0:1], psO[0:16, 0:1]),
           reads=["psO"], writes=["pcol"], tag="p0 pare")
    P.step("V", lambda v: v.tensor_scalar(pcol[:, 1:2], pcol[:, 0:1], 0.0,
                                          None, ALU.is_equal),
           reads=["pcol"], writes=["pcol"], tag="p0 parn")
    P.step("V", lambda v: v.memset(u8acc, 0.0), writes=["u8acc"], tag="p0 u8z")

    P.barrier()

    def ln_stats_part(src_d, src_name, row0, tag, tc):
        hb = tc % 2
        rows = slice(row0 + tc * 128, row0 + (tc + 1) * 128)
        dma("SY", xst, src_d[rows, :], [src_name], ["xst"], f"{tag} x{tc}")
        P.step("V", lambda v: v.tensor_reduce(
            mcol[:, 0:1], xst, mybir.AxisListType.X, ALU.add),
            reads=["xst"], writes=["mcol"], tag=f"{tag} sum{tc}")
        P.step("SC", lambda s: s.activation(
            xsq, xst, AF.Square, accum_out=mcol[:, 1:2]),
            reads=["xst"], writes=["xsq", "mcol"], tag=f"{tag} sq{tc}")

        def stats(v):
            v.tensor_scalar(mcol[:, 0:1], mcol[:, 0:1], 1.0 / D, None, ALU.mult)
            v.drain()
            v.tensor_scalar(mcol[:, 2:3], mcol[:, 1:2], 1.0 / D, None, ALU.mult)
            v.drain()
            v.tensor_tensor(mcol[:, 3:4], mcol[:, 0:1], mcol[:, 0:1], ALU.mult)
            v.drain()
            v.tensor_tensor(mcol[:, 2:3], mcol[:, 2:3], mcol[:, 3:4], ALU.subtract)
            v.drain()
            v.tensor_scalar(mcol[:, 2:3], mcol[:, 2:3], EPS, None, ALU.add)
            v.drain()
            return v.reciprocal(mcol[:, 4:5], mcol[:, 2:3])
        P.step("V", stats, reads=["mcol"], writes=["mcol"], tag=f"{tag} st{tc}")
        P.step("SC", lambda s: s.activation(mcol[:, 5:6], mcol[:, 4:5], AF.Sqrt),
               reads=["mcol"], writes=["mcol"], tag=f"{tag} rs{tc}")
        P.step("V", lambda v: v.tensor_scalar(
            xsq, xst, mcol[:, 0:1], mcol[:, 5:6], ALU.subtract, ALU.mult),
            reads=["xst", "mcol", "xsq"], writes=["xsq"], tag=f"{tag} xh{tc}")
        P.step("V", lambda v, hb=hb: v.tensor_copy(xhh[hb], xsq),
               reads=["xsq"], writes=[f"xhh{hb}"], tag=f"{tag} hh{tc}")
        P.step("V", lambda v, hb=hb: v.scalar_tensor_tensor(
            xhl[hb], xsq, 1.0, xhh[hb], ALU.mult, ALU.subtract),
            reads=["xsq", f"xhh{hb}"], writes=[f"xhl{hb}"], tag=f"{tag} hl{tc}")

    def ln_transpose_part(tag, tc, buf):
        hb = tc % 2
        xfh_t, xfl_t, nxh, nxl = XF[buf]

        def trh(te, hb=hb):
            last = None
            for dc in range(NDC):
                last = te.transpose(psT[:, dc * 128:(dc + 1) * 128],
                                    xhh[hb][:, dc * 128:(dc + 1) * 128], identbf)
            return last
        P.step("T", trh, reads=[f"xhh{hb}", "identbf", "psT"], writes=["psT"],
               tag=f"{tag} th{tc}")
        P.step("SC", lambda s, tc=tc, xf=xfh_t: s.activation(
            xf.rearrange("p (dc t) -> p dc t", dc=NDC)[:, :, tc * 128:(tc + 1) * 128],
            psT.rearrange("p (dc t) -> p dc t", dc=NDC), AF.Copy),
            reads=["psT"], writes=[nxh], tag=f"{tag} eh{tc}")

        def trl(te, hb=hb):
            last = None
            for dc in range(NDC):
                last = te.transpose(psT[:, dc * 128:(dc + 1) * 128],
                                    xhl[hb][:, dc * 128:(dc + 1) * 128], identbf)
            return last
        P.step("T", trl, reads=[f"xhl{hb}", "identbf", "psT"], writes=["psT"],
               tag=f"{tag} tl{tc}")
        P.step("SC", lambda s, tc=tc, xf=xfl_t: s.activation(
            xf.rearrange("p (dc t) -> p dc t", dc=NDC)[:, :, tc * 128:(tc + 1) * 128],
            psT.rearrange("p (dc t) -> p dc t", dc=NDC), AF.Copy),
            reads=["psT"], writes=[nxl], tag=f"{tag} el{tc}")

    def ln_split_transpose(src_d, src_name, row0, tag, buf=0):
        for tc in range(NTC):
            ln_stats_part(src_d, src_name, row0, tag, tc)
            ln_transpose_part(tag, tc, buf)

    def k_proj_part(oc, whi_t, wlo_t, whin, wlon, tag, buf=0):
        b = oc % NKST
        ps, psn = (psA, "psA") if b == 0 else (psB, "psB")
        xfh_t, xfl_t, nxh, nxl = XF[buf]

        def kproj(te, oc=oc, ps=ps):
            last = None
            for dc in range(NDC):
                osl = slice(dc * D + oc * 128, dc * D + oc * 128 + 128)
                tsl = slice(dc * SW, (dc + 1) * SW)
                last = te.matmul(ps[:, 0:SW], whi_t[:, osl], xfh_t[:, tsl],
                                 start=(dc == 0), stop=False)
                last = te.matmul(ps[:, 0:SW], whi_t[:, osl], xfl_t[:, tsl],
                                 start=False, stop=False)
                last = te.matmul(ps[:, 0:SW], wlo_t[:, osl], xfh_t[:, tsl],
                                 start=False, stop=(dc == NDC - 1))
            return last
        P.step("T", kproj, reads=[whin, wlon, nxh, nxl], writes=[psn],
               tag=f"{tag} proj{oc}")
        P.step("SC", lambda s, ps=ps, b=b: s.activation(
            kh_st[b], ps[:, 0:SW], AF.Copy),
            reads=[psn], writes=[f"kh_st{b}"], tag=f"{tag} ev{oc}")
        P.step("V", lambda v, ps=ps, b=b: v.scalar_tensor_tensor(
            kl_st[b], ps[:, 0:SW], 1.0, kh_st[b], ALU.mult, ALU.subtract),
            reads=[psn, f"kh_st{b}"], writes=[f"kl_st{b}"], tag=f"{tag} lo{oc}")

    def k_log_part(oc, tag):
        b = oc % NKST

        def klog(te, oc=oc, b=b):
            last = None
            for tc in range(NTC):
                tsl = slice(tc * 128, (tc + 1) * 128)
                lsl = slice(tc * NB, (tc + 1) * NB)
                whh = wh_hi[:, oc * NB:(oc + 1) * NB]
                whl = wh_lo[:, oc * NB:(oc + 1) * NB]
                last = te.matmul(psO[:, lsl], kh_st[b][:, tsl], whh,
                                 start=True, stop=False)
                last = te.matmul(psO[:, lsl], kh_st[b][:, tsl], whl,
                                 start=False, stop=False)
                last = te.matmul(psO[:, lsl], kl_st[b][:, tsl], whh,
                                 start=False, stop=True)
            return last
        P.step("T", klog, reads=[f"kh_st{b}", f"kl_st{b}", "wh_hi", "wh_lo"],
               writes=["psO"], tag=f"{tag} lg{oc}")
        if oc == 0:
            P.step("SC", lambda s: s.activation(lg32, psO[:, 0:NTC * NB],
                                                AF.Copy),
                   reads=["psO"], writes=["lg32"], tag=f"{tag} lga{oc}")
        else:
            P.step("V", lambda v: v.tensor_tensor(lg32, lg32,
                                                  psO[:, 0:NTC * NB], ALU.add),
                   reads=["psO", "lg32"], writes=["lg32"], tag=f"{tag} lga{oc}")

    def u_compute(u_target, u_name, tag):
        for tc in range(NTC):
            lsl = slice(tc * NB, (tc + 1) * NB)

            def ufn(v, lsl=lsl):
                v.max(m8, lg32[:, lsl])
                v.drain()
                v.max_index(mi8, m8, lg32[:, lsl])
                v.drain()
                v.tensor_copy(mif, mi8)
                v.drain()
                return v.tensor_scalar(u8t, iotaF[:, 0:NB], mif[:, 0:1], 8.0,
                                       ALU.is_equal, ALU.mult)
            P.step("V", ufn, reads=["lg32", "iotaF"],
                   writes=["u8t", "m8", "mi8", "mif"], tag=f"{tag} u{tc}")
            P.step("T", lambda te: te.transpose(psT[0:16, 0:128], u8t, identbf),
                   reads=["u8t", "identbf", "psT"], writes=["psT"],
                   tag=f"{tag} ut{tc}")
            P.step("V", lambda v, tc=tc: v.tensor_copy(
                u_target[:, tc * 128:(tc + 1) * 128], psT[0:16, 0:128]),
                reads=["psT"], writes=[u_name], tag=f"{tag} ue{tc}")

    wkhi, wklo, wvhi, wvlo = AR1, AR2, AR3, AR4

    ln_split_transpose(x_full, "x_full", 0, "p1.0", buf=0)

    for wi, (wd, whi_t, wlo_t, nmw) in enumerate(
            ((wk_d, wkhi, wklo, "wk"), (wv_d, wvhi, wvlo, "wv"))):
        for dc in range(NDC):
            slot = (dc % 2) + 2 * wi
            st32 = AR5[:, slot * 1024:slot * 1024 + 1024]
            dst_hi = whi_t.rearrange("p (dc o) -> p dc o", dc=NDC)[:, dc, :]
            dst_lo = wlo_t.rearrange("p (dc o) -> p dc o", dc=NDC)[:, dc, :]
            nm32 = f"w32s_{slot}"
            dma("SY", st32, wd[dc * 128:(dc + 1) * 128, :], ["i"], [nm32],
                f"p1 w32 {nmw}{dc}")
            P.step("SC", lambda s, a=dst_hi, b=st32: s.activation(
                a, b, AF.Copy), reads=[nm32, nmw + "_hi"], writes=[nmw + "_hi"],
                tag=f"p1 whi {nmw}{dc}")
            P.step("V", lambda v, a=st32, b=dst_hi, c=dst_lo:
                   v.scalar_tensor_tensor(c, a, 1.0, b, ALU.mult, ALU.subtract),
                   reads=[nm32, nmw + "_hi"], writes=[nmw + "_lo"],
                   tag=f"p1 wlo {nmw}{dc}")

    for vb in range(2):
        vv3g = vst[vb].rearrange("p (h c) -> p h c", h=H)
        P.step("V", lambda v, a=vv3g[:, :, 64:65]: v.memset(a, 1.0),
               writes=[f"vst{vb}"], tag=f"p1 vones{vb}")
        P.step("V", lambda v, a=vv3g[:, :, 129:130]: v.memset(a, 0.0),
               writes=[f"vst{vb}"], tag=f"p1 vzer{vb}")
    for tau in range(NTAU):
        cols = slice(tau * SW, (tau + 1) * SW)
        buf = tau % 2

        def k_scratch_dma(oc):
            b = oc % NKST
            for hh, base in ((2 * oc, 0), (2 * oc + 1, 64)):
                sl = slice(base, base + 64)
                dma("SY", KA_d[hh, 0:64, cols], kh_st[b][sl, :],
                    [f"kh_st{b}"], [f"KA_d#{tau}.{hh}.a"], f"p1 ka {tau}.{oc}")
                dma("SY", KA_d[hh, 80:144, cols], kl_st[b][sl, :],
                    [f"kl_st{b}"], [f"KA_d#{tau}.{hh}.b"], f"p1 kb {tau}.{oc}")

        for oc in range(NDC):
            k_proj_part(oc, wkhi, wklo, "wk_hi", "wk_lo", f"p1K.{tau}", buf=buf)
            if oc > 0:
                k_log_part(oc - 1, f"p1K.{tau}")
                k_scratch_dma(oc - 1)
            if tau + 1 < NTAU:
                if oc % 2 == 1 and oc >= 3 and (oc - 2) // 2 < NTC:
                    ln_transpose_part(f"p1.{tau + 1}", (oc - 2) // 2,
                                      (tau + 1) % 2)
                if oc % 2 == 1 and oc // 2 < NTC:
                    ln_stats_part(x_full, "x_full", (tau + 1) * SW,
                                  f"p1.{tau + 1}", oc // 2)
            else:
                if oc % 2 == 1 and oc >= 3 and (oc - 2) // 2 < NTC:
                    ln_transpose_part("p15.0", (oc - 2) // 2, 0)
                if oc % 2 == 1 and oc // 2 < NTC:
                    ln_stats_part(x_own, "x_own", 0, "p15.0", oc // 2)
        k_log_part(NDC - 1, f"p1K.{tau}")
        k_scratch_dma(NDC - 1)
        if tau + 1 < NTAU:
            for tcr in range(max(0, min(NTC, (NDC - 2) // 2)), NTC):
                ln_transpose_part(f"p1.{tau + 1}", tcr, (tau + 1) % 2)
        else:
            for tcr in range(max(0, min(NTC, (NDC - 2) // 2)), NTC):
                ln_transpose_part("p15.0", tcr, 0)

        u_compute(u8f, "u8f", f"p1U.{tau}")
        for hh in range(H):
            dma("GD", KA_d[hh, 64:80, cols], u8f, ["u8f"],
                [f"KA_d#{tau}.{hh}.u"], f"p1 ua {tau}.{hh}")
        osl_u = slice((tau // 2) * SW, (tau // 2) * SW + SW)
        mci = 1 if tau % 2 == 0 else 0
        P.step("V", lambda v, osl_u=osl_u, mci=mci: v.scalar_tensor_tensor(
            u8acc[:, osl_u], u8f, pcol[:, mci:mci + 1], u8acc[:, osl_u],
            ALU.mult, ALU.add),
            reads=["u8f", "pcol", "u8acc"], writes=["u8acc"],
            tag=f"p1 uacc {tau}")

        for tc in range(NTC):
            for ot in range(NOT):
                i = tc * NOT + ot
                ps, psn = (psC, "psC") if i % 2 == 0 else (psD, "psD")

                xfh_t, xfl_t, nxh, nxl = XF[buf]

                def vproj(te, tc=tc, ot=ot, ps=ps, xfh_t=xfh_t, xfl_t=xfl_t):
                    last = None
                    for dc in range(NDC):
                        xsl = slice(dc * SW + tc * 128, dc * SW + tc * 128 + 128)
                        wsl = slice(dc * D + ot * 512, dc * D + ot * 512 + 512)
                        last = te.matmul(ps, xfh_t[:, xsl], wvhi[:, wsl],
                                         start=(dc == 0), stop=False)
                        last = te.matmul(ps, xfl_t[:, xsl], wvhi[:, wsl],
                                         start=False, stop=False)
                        last = te.matmul(ps, xfh_t[:, xsl], wvlo[:, wsl],
                                         start=False, stop=(dc == NDC - 1))
                    return last
                P.step("T", vproj, reads=[nxh, nxl, "wv_hi", "wv_lo"],
                       writes=[psn], tag=f"p1V {tau}.{tc}.{ot}")
                hsl = slice(ot * 8, (ot + 1) * 8)
                vb = tc % 2
                vv3 = vst[vb].rearrange("p (h c) -> p h c", h=H)
                vhv = vv3[:, hsl, 0:64]
                vlv = vv3[:, hsl, 65:129]
                psv = ps.rearrange("p (h c) -> p h c", h=8)
                P.step("SC", lambda s, a=vhv, b=psv: s.activation(a, b, AF.Copy),
                       reads=[psn], writes=[f"vst{vb}"],
                       tag=f"p1V ev {tau}.{tc}.{ot}")
                P.step("V", lambda v, a=vlv, b=psv, c=vhv:
                       v.scalar_tensor_tensor(a, b, 1.0, c, ALU.mult, ALU.subtract),
                       reads=[psn, f"vst{vb}"], writes=[f"vst{vb}"],
                       tag=f"p1V lo {tau}.{tc}.{ot}")
            rows = slice(tau * SW + tc * 128, tau * SW + (tc + 1) * 128)
            vb = tc % 2
            dst = V_d[:, rows, :].transpose([1, 0, 2])
            dma("SY", dst, vst[vb].rearrange("p (h c) -> p h c", h=H),
                [f"vst{vb}"], [f"V_d#{tau}.{tc}"], f"p1 vd {tau}.{tc}")

    wqhi, wqlo = AR3, AR4
    for dc in range(NDC):
        slot = dc % 2
        st32 = AR5[:, slot * 1024:slot * 1024 + 1024]
        dst_hi = wqhi.rearrange("p (dc o) -> p dc o", dc=NDC)[:, dc, :]
        dst_lo = wqlo.rearrange("p (dc o) -> p dc o", dc=NDC)[:, dc, :]
        nm32 = f"w32s_{slot}"
        dma("SY", st32, wq_d[dc * 128:(dc + 1) * 128, :], ["i"], [nm32],
            f"p15 wq32 {dc}")
        P.step("SC", lambda s, a=dst_hi, b=st32: s.activation(
            a, b, AF.Copy, scale=0.125), reads=[nm32, "wq_hi"],
            writes=["wq_hi", "wv_hi"], tag=f"p15 wqh {dc}")
        P.step("V", lambda v, a=st32, b=dst_hi, c=dst_lo:
               v.scalar_tensor_tensor(c, a, 0.125, b, ALU.mult, ALU.subtract),
               reads=[nm32, "wq_hi"], writes=["wq_lo", "wv_lo"],
               tag=f"p15 wql {dc}")

    P.barrier()

    for hh in range(H):
        dma("GD", QA_d[hh, 64:80, :], u8acc, ["u8acc"],
            [f"QA_d#u.{hh}"], f"p15 ua {hh}")

    for j in range(NSTR):
        srow = j * SW
        qcols = slice(srow, srow + SW)
        buf = j % 2
        for oc in range(NDC):
            if j + 1 < NSTR:
                if oc % 2 == 1 and oc >= 3 and (oc - 2) // 2 < NTC:
                    ln_transpose_part(f"p15.{j + 1}", (oc - 2) // 2,
                                      (j + 1) % 2)
                if oc % 2 == 1 and oc // 2 < NTC:
                    ln_stats_part(x_own, "x_own", (j + 1) * SW,
                                  f"p15.{j + 1}", oc // 2)
            b = oc % 2
            ps, psn = (psA, "psA") if b == 0 else (psB, "psB")

            xfh_t, xfl_t, nxh, nxl = XF[buf]

            def qproj(te, oc=oc, ps=ps, xfh_t=xfh_t, xfl_t=xfl_t):
                last = None
                for dc in range(NDC):
                    osl = slice(dc * D + oc * 128, dc * D + oc * 128 + 128)
                    tsl = slice(dc * SW, (dc + 1) * SW)
                    last = te.matmul(ps[:, 0:SW], wqhi[:, osl], xfh_t[:, tsl],
                                     start=(dc == 0), stop=False)
                    last = te.matmul(ps[:, 0:SW], wqhi[:, osl], xfl_t[:, tsl],
                                     start=False, stop=False)
                    last = te.matmul(ps[:, 0:SW], wqlo[:, osl], xfh_t[:, tsl],
                                     start=False, stop=(dc == NDC - 1))
                return last
            P.step("T", qproj, reads=["wq_hi", "wq_lo", nxh, nxl], writes=[psn],
                   tag=f"p15 Q {j}.{oc}")
            P.step("SC", lambda s, ps=ps, b=b: s.activation(
                kh_st[b], ps[:, 0:SW], AF.Copy),
                reads=[psn], writes=[f"kh_st{b}"], tag=f"p15 qe {j}.{oc}")
            P.step("V", lambda v, ps=ps, b=b: v.scalar_tensor_tensor(
                kl_st[b], ps[:, 0:SW], 1.0, kh_st[b], ALU.mult, ALU.subtract),
                reads=[psn, f"kh_st{b}"], writes=[f"kl_st{b}"],
                tag=f"p15 ql {j}.{oc}")
            for hh, base in ((2 * oc, 0), (2 * oc + 1, 64)):
                sl = slice(base, base + 64)
                dma("SY", QA_d[hh, 0:64, qcols], kh_st[b][sl, 0:SW],
                    [f"kh_st{b}"], [f"QA_d#{j}.{hh}.a"], f"p15 qa {j}.{hh}")
                dma("SY", QA_d[hh, 80:144, qcols], kl_st[b][sl, 0:SW],
                    [f"kl_st{b}"], [f"QA_d#{j}.{hh}.b"], f"p15 qb {j}.{hh}")
        if j + 1 < NSTR:
            for tcr in range(max(0, min(NTC, (NDC - 2) // 2)), NTC):
                ln_transpose_part(f"p15.{j + 1}", tcr, (j + 1) % 2)

    P.barrier()

    OHt, OLt = AR3, AR4
    KG = 8
    KAb = [SB(f"KAb{i}", [80, KG * 128], BF16) for i in range(2)]
    KBb = [SB(f"KBb{i}", [128, KG * 128], BF16) for i in range(2)]
    Vb = [SB(f"Vb{i}", [128, KG * 130], BF16) for i in range(2)]
    QAb = [SB(f"QAb{i}", [80, SW], BF16) for i in range(2)]
    QBb = [SB(f"QBb{i}", [128, SW], BF16) for i in range(2)]
    E32 = [SB(f"E32_{i}", [128, 512]) for i in range(2)]
    PE32 = [AR2[:, i * 1024:i * 1024 + 1024].bitcast(F32) for i in range(4)]
    Phi = [AR2[:, 4096 + i * 512:4096 + i * 512 + 512] for i in range(4)]
    Plo = [AR2[:, 6144 + i * 512:6144 + i * 512 + 512] for i in range(4)]
    O32 = AR5[0:64, 0:SW]
    RB32 = AR5[0:64, 512:512 + SW]
    Oh64 = AR5[0:64, 1024:1024 + SW // 2].bitcast(BF16)
    Ol64 = AR5[0:64, 1536:1536 + SW // 2].bitcast(BF16)
    rl32 = AR5[0:1, 3072:3072 + SW]
    rlhi = AR5[0:1, 3584:3584 + SW // 2].bitcast(BF16)
    rllo = AR5[0:1, 3584 + SW // 2:3584 + SW].bitcast(BF16)
    QP = SB("QP", [128, SW])
    CBIAS = AR1[:, 0:8 * SW]
    wo_hi = AR3[:, 4096:4096 + NDC * 512]
    wo_lo = AR4[:, 4096:4096 + NDC * 512]
    xres = AR7
    x1st = AR5[:, 2048:3072]

    for j in range(NSTR):
        R = cfg.rng_chunks(j)
        srow = j * SW

        def qpb(te, srow=srow):
            te.matmul(psD[:, 0:SW], ones1[:, 0:128],
                      qpos_hi[0:1, srow:srow + SW], start=True, stop=False)
            return te.matmul(psD[:, 0:SW], ones1[:, 0:128],
                             qpos_lo[0:1, srow:srow + SW],
                             start=False, stop=True)
        P.step("T", qpb, reads=["qpos_hi", "qpos_lo", "ones1", "psD"],
               writes=["psD"], tag=f"p2 qp {j}")
        P.step("SC", lambda s: s.activation(QP[:, 0:SW], psD[:, 0:SW], AF.Copy),
               reads=["psD"], writes=["QP"], tag=f"p2 qpe {j}")
        for e in range(8):
            kc = R - 8 + e
            if kc < 0:
                continue

            def cb(v, kc=kc, e=e):
                v.tensor_scalar(kcc, kcol, float(kc * 128), None, ALU.add)
                v.drain()
                return v.tensor_scalar(CBIAS[:, e * SW:(e + 1) * SW],
                                       QP[:, 0:SW], kcc[:, 0:1], -CBIG,
                                       ALU.is_lt, ALU.mult)
            P.step("V", cb, reads=["QP", "kcol"], writes=["CBIAS", "kcc"],
                   tag=f"p2 cb {j}.{e}")

        if j == DBGJ:
            dma("GD", dbgC[:, 6 * SW:8 * SW], CBIAS[:, 6 * SW:8 * SW], ["CBIAS"], ["dbgC"], "dbg cb")
            dma("GD", dbgP[:, 0:SW], QP[:, 0:SW], ["QP"], ["dbgP"], "dbg qp")
            P.step("GD", lambda raw: raw.dma_start(out=dbgP[:, SW:SW + 1], in_=kcc, allow_slow_non_contiguous=True), reads=["kcc"], writes=["dbgP"], tag="dbg kcc")
        pend = []
        for hh in range(H):
            qbuf = hh % 2
            qa, qb = QAb[qbuf], QBb[qbuf]
            nqa, nqb = f"QAb{qbuf}", f"QBb{qbuf}"
            qcols = slice(srow, srow + SW)
            dma("SY", qa, QA_d[hh, 0:80, qcols], ["QA_d", nqa], [nqa],
                f"p2 lqa {j}.{hh}")
            dma("SY", qb[0:64, :], QA_d[hh, 0:64, qcols], ["QA_d", nqb],
                [nqb + "#a"], f"p2 lqb {j}.{hh}")
            dma("SY", qb[64:128, :], QA_d[hh, 80:144, qcols], ["QA_d", nqb],
                [nqb + "#b"], f"p2 lqb2 {j}.{hh}")
            ngrp = (R + KG - 1) // KG
            psS = [(psA, "psA"), (psB, "psB"), (psC, "psC"), (psD, "psD")]
            LAG = 3

            def emit_qk(kc):
                g = kc // KG
                buf = g % 2
                ka, kb, vv = KAb[buf], KBb[buf], Vb[buf]
                nka, nkb, nv = f"KAb{buf}", f"KBb{buf}", f"Vb{buf}"
                if kc % KG == 0:
                    g0, g1 = g * KG * 128, min(R, (g + 1) * KG) * 128
                    gn = (g1 - g0) // 128
                    dma("SY", ka[:, 0:g1 - g0], KA_d[hh, 0:80, g0:g1],
                        ["KA_d", nka], [nka], f"p2 lka {j}.{hh}.{g}")
                    dma("SY", kb[0:64, 0:g1 - g0], KA_d[hh, 80:144, g0:g1],
                        ["KA_d", nkb], [nkb + "#a"], f"p2 lkb {j}.{hh}.{g}")
                    dma("SY", kb[64:128, 0:g1 - g0], KA_d[hh, 0:64, g0:g1],
                        ["KA_d", nkb], [nkb + "#b"], f"p2 lkb2 {j}.{hh}.{g}")
                    dma("SY", vv.rearrange("p (k c) -> p k c", c=130)[:, 0:gn, :],
                        V_d[hh, g0:g1, :].rearrange("(k p) c -> p k c", p=128),
                        ["V_d", nv], [nv], f"p2 lv {j}.{hh}.{g}")
                pb = kc % 4
                ps, psn = psS[pb]
                ksl = slice((kc % KG) * 128, (kc % KG) * 128 + 128)

                e = kc - (R - 8)

                def qk(te, ka=ka, kb=kb, ksl=ksl, ps=ps, qa=qa, qb=qb, e=e):
                    te.matmul(ps[:, 0:SW], ka[:, ksl], qa[0:80, :],
                              start=True, stop=False)
                    if e >= 0:
                        te.matmul(ps[:, 0:SW], kb[:, ksl], qb,
                                  start=False, stop=False)
                        return te.matmul(ps[:, 0:SW], identbf,
                                         CBIAS[:, e * SW:(e + 1) * SW],
                                         start=False, stop=True)
                    return te.matmul(ps[:, 0:SW], kb[:, ksl], qb,
                                     start=False, stop=True)
                P.step("T", qk, reads=[nka, nkb + "#a", nkb + "#b", nqa, nqb,
                                       "identbf", "CBIAS"],
                       writes=[psn], tag=f"p2 qk {j}.{hh}.{kc}")
                P.step("SC", lambda s, ps=ps, pb=pb: s.activation(
                    PE32[pb][:, 0:SW], ps[:, 0:SW], AF.Exp,
                    bias=nalpha[:, 0:1]),
                    reads=[psn, "nalpha"], writes=[f"PE32_{pb}"],
                    tag=f"p2 exf {j}.{hh}.{kc}")
                P.step("V", lambda v, pb=pb: v.tensor_copy(
                    Phi[pb][:, 0:SW].bitcast(BF16), PE32[pb][:, 0:SW]),
                    reads=[f"PE32_{pb}"], writes=[f"Phi{pb}"],
                    tag=f"p2 exh {j}.{hh}.{kc}")
                P.step("V", lambda v, pb=pb: v.scalar_tensor_tensor(
                    Plo[pb][:, 0:SW].bitcast(BF16), PE32[pb][:, 0:SW], 1.0,
                    Phi[pb][:, 0:SW].bitcast(BF16), ALU.mult, ALU.subtract),
                    reads=[f"PE32_{pb}", f"Phi{pb}"], writes=[f"Plo{pb}"],
                    tag=f"p2 pl {j}.{hh}.{kc}")

            def emit_pv(kc):
                g = kc // KG
                buf = g % 2
                vv = Vb[buf]
                nv = f"Vb{buf}"
                pb = kc % 4
                kg = kc % KG

                def pva(te, vv=vv, kg=kg, pb=pb, first=(kc == 0)):
                    vva = vv.rearrange("p (k c) -> p k c", c=130)
                    te.matmul(psO[0:65, 0:SW], vva[:, kg, 0:65],
                              Phi[pb][:, 0:SW].bitcast(BF16),
                              start=first, stop=False)
                    return te.matmul(psO[0:65, 0:SW], vva[:, kg, 65:130],
                                     Phi[pb][:, 0:SW].bitcast(BF16),
                                     start=False, stop=False)
                P.step("T", pva, reads=[nv, f"Phi{pb}", "psO"], writes=["psO"],
                       tag=f"p2 pva {j}.{hh}.{kc}")

                def pvb(te, vv=vv, kg=kg, pb=pb, last=(kc == R - 1)):
                    vva = vv.rearrange("p (k c) -> p k c", c=130)
                    return te.matmul(psO[0:65, 0:SW], vva[:, kg, 0:65],
                                     Plo[pb][:, 0:SW].bitcast(BF16),
                                     start=False, stop=last)
                P.step("T", pvb, reads=[nv, f"Plo{pb}", "psO"], writes=["psO"],
                       tag=f"p2 pvb {j}.{hh}.{kc}")

            for it in range(R + LAG):
                if it < R:
                    emit_qk(it)
                if it >= LAG:
                    emit_pv(it - LAG)

            def rfin(v):
                v.tensor_copy(rlhi[:, 0:SW], psO[64:65, 0:SW])
                v.drain()
                return v.scalar_tensor_tensor(rllo[:, 0:SW], psO[64:65, 0:SW],
                                              1.0, rlhi[:, 0:SW],
                                              ALU.mult, ALU.subtract)
            P.step("V", rfin, reads=["psO"], writes=["rlhi", "rllo"],
                   tag=f"p2 rf {j}.{hh}")

            def rb(te):
                te.matmul(psD[0:64, 0:SW], ones1[:, 0:64], rlhi[0:1, 0:SW],
                          start=True, stop=False)
                return te.matmul(psD[0:64, 0:SW], ones1[:, 0:64],
                                 rllo[0:1, 0:SW], start=False, stop=True)
            P.step("T", rb, reads=["rlhi", "rllo", "ones1", "psD"],
                   writes=["psD"], tag=f"p2 rbb {j}.{hh}")
            pair, half = divmod(hh, 2)
            oh = OHt[:, pair * SW:(pair + 1) * SW]
            ol = OLt[:, pair * SW:(pair + 1) * SW]
            hsl = slice(half * 64, half * 64 + 64)
            P.step("V", lambda v: v.reciprocal(RB32[:, 0:SW], psD[0:64, 0:SW]),
                   reads=["psD"], writes=["RB32"], tag=f"p2 rbe {j}.{hh}")
            P.step("V", lambda v: v.tensor_tensor(
                O32[:, 0:SW], psO[0:64, 0:SW], RB32[:, 0:SW], ALU.mult),
                reads=["psO", "RB32"], writes=["O32"], tag=f"p2 on {j}.{hh}")
            P.step("SC", lambda s: s.activation(Oh64[:, 0:SW], O32[:, 0:SW],
                                                AF.Copy),
                   reads=["O32"], writes=["Oh64"], tag=f"p2 oh {j}.{hh}")
            P.step("V", lambda v: v.scalar_tensor_tensor(
                Ol64[:, 0:SW], O32[:, 0:SW], 1.0, Oh64[:, 0:SW], ALU.mult,
                ALU.subtract),
                reads=["O32", "Oh64"], writes=["Ol64"], tag=f"p2 ol {j}.{hh}")
            dma("GD", oh[hsl, :], Oh64[:, 0:SW], ["Oh64", f"OH#{hh}"],
                [f"OH#{hh}"], f"p2 ohd {j}.{hh}")
            dma("GD", ol[hsl, :], Ol64[:, 0:SW], ["Ol64", f"OL#{hh}"],
                [f"OL#{hh}"], f"p2 old {j}.{hh}")
            if j == DBGJ:
                dma("GD", dbgO[hh], O32[:, 0:SW], ["O32"], ["dbgO"],
                    f"dbg o {hh}")

        for tc in range(NTC):
            rows = slice(srow + tc * 128, srow + (tc + 1) * 128)
            dma("SY", xres[:, tc * D:(tc + 1) * D], x_own[rows, :],
                ["x_own"], ["xres"], f"p2 xr {j}.{tc}")
        for ot in range(NOT):
            for cc in range(NDC):
                slot = cc % 2
                st32 = AR6[:, 2048 + slot * 512:2048 + slot * 512 + 512]
                srcw = wo_d.rearrange("(cc p) o -> p cc o", p=128)[
                    :, cc, ot * 512:(ot + 1) * 512]
                nmw = f"wo32_{slot}"
                dma("SY", st32, srcw, ["i", nmw], [nmw], f"p2 wo {j}.{ot}.{cc}")
                whs = wo_hi[:, cc * 512:(cc + 1) * 512]
                wls = wo_lo[:, cc * 512:(cc + 1) * 512]
                P.step("SC", lambda s, a=whs, b=st32: s.activation(a, b, AF.Copy),
                       reads=[nmw, "wo_hi"], writes=["wo_hi"],
                       tag=f"p2 woh {j}.{ot}.{cc}")
                P.step("V", lambda v, a=wls, b=st32, c=whs:
                       v.scalar_tensor_tensor(a, b, 1.0, c, ALU.mult,
                                              ALU.subtract),
                       reads=[nmw, "wo_hi", "wo_lo"], writes=["wo_lo"],
                       tag=f"p2 wol {j}.{ot}.{cc}")
            woh = wo_hi.rearrange("p (cc o) -> p cc o", cc=NDC)
            wol = wo_lo.rearrange("p (cc o) -> p cc o", cc=NDC)
            for tc in range(NTC):
                ps, psn = (psC, "psC") if tc % 2 == 0 else (psD, "psD")

                def wog(te, tc=tc, ps=ps):
                    last = None
                    for cc in range(NDC):
                        tsl = slice(cc * SW + tc * 128, cc * SW + tc * 128 + 128)
                        last = te.matmul(ps, OHt[:, tsl], woh[:, cc, :],
                                         start=(cc == 0), stop=False)
                        last = te.matmul(ps, OLt[:, tsl], woh[:, cc, :],
                                         start=False, stop=False)
                        last = te.matmul(ps, OHt[:, tsl], wol[:, cc, :],
                                         start=False, stop=(cc == NDC - 1))
                    return last
                P.step("T", wog, reads=["OH", "OL", "wo_hi", "wo_lo", psn],
                       writes=[psn], tag=f"p2 wog {j}.{ot}.{tc}")
                P.step("V", lambda v, ps=ps, tc=tc, ot=ot: v.tensor_tensor(
                    x1st[:, ot * 512:(ot + 1) * 512], ps,
                    xres[:, tc * D + ot * 512:tc * D + ot * 512 + 512], ALU.add),
                    reads=[psn, "xres", "x1st"], writes=["x1st"],
                    tag=f"p2 res {j}.{ot}.{tc}")
                rows = slice(srow + tc * 128, srow + (tc + 1) * 128)
                dma("GD", x1_d[rows, ot * 512:(ot + 1) * 512],
                    x1st[:, ot * 512:(ot + 1) * 512], ["x1st"],
                    [f"x1_d#{j}.{ot}.{tc}"], f"p2 x1 {j}.{ot}.{tc}")

    P.barrier()

    W1h = AR1[:, 0:3072]
    W1l = AR2[:, 0:3072]
    W2h = AR3
    W2l = AR4
    hh_t = AR1[:, 3072:3072 + FB * SW]
    hl_t = AR2[:, 3072:3072 + FB * SW]
    hr32 = E32
    x1t = AR7
    g32 = SB("g32", [128, 8])
    eg = SB("eg", [128, NE_R])
    w4 = SB("w4", [128, 8])
    c4 = SB("c4", [128, NE_R])
    c4all = SB("c4all", [128, NTC * NE_R])
    c4h = SB("c4h", [128, NE_R], BF16)
    c4l = SB("c4l", [128, NE_R], BF16)
    c6h = SB("c6h", [6, SW], BF16)
    c6l = SB("c6l", [6, SW], BF16)
    oh1 = SB("oh1", [128, NE_R])
    oh2 = SB("oh2", [128, NE_R])

    for tile in range(NSTR):
        trow = tile * SW
        ln_split_transpose(x1_d, "x1_d", trow, f"p3.{tile}")
        for tc in range(NTC):
            rows = slice(trow + tc * 128, trow + (tc + 1) * 128)
            dma("SY", x1t[:, tc * D:(tc + 1) * D], x1_d[rows, :],
                ["x1_d", "x1t"], ["x1t"], f"p3 x1t {tile}.{tc}")

        P.step("V", lambda v: v.memset(c6h, 0.5), writes=["c6h"],
               tag=f"p3 c6s {tile}")
        P.step("V", lambda v: v.memset(c6l, 0.0), writes=["c6l"],
               tag=f"p3 c6z {tile}")
        P.step("V", lambda v: v.memset(g32[:, NE_R:8], -3.0e38),
               reads=[], writes=["g32"], tag=f"p3 gpad {tile}")
        P.step("V", lambda v: v.memset(w4[:, NE_R:8], -3.0e38),
               reads=[], writes=["w4"], tag=f"p3 wpad {tile}")
        for tc in range(NTC):
            gsl = slice(tc * NE_R, (tc + 1) * NE_R)

            def gmm(te, tc=tc, gsl=gsl):
                for dc in range(NDC):
                    tsl = slice(dc * SW + tc * 128, dc * SW + tc * 128 + 128)
                    gh = gw_hi[:, dc * NE_R:(dc + 1) * NE_R]
                    gl = gw_lo[:, dc * NE_R:(dc + 1) * NE_R]
                    te.matmul(psO[:, gsl], xfh[:, tsl], gh,
                              start=(dc == 0), stop=False)
                    te.matmul(psO[:, gsl], xfh[:, tsl], gl, start=False, stop=False)
                    te.matmul(psO[:, gsl], xfl[:, tsl], gh, start=False, stop=False)
                te.matmul(psO[:, gsl], ones1[:, 0:128], gb_hi[0:1, :],
                          start=False, stop=False)
                return te.matmul(psO[:, gsl], ones1[:, 0:128], gb_lo[0:1, :],
                                 start=False, stop=True)
            P.step("T", gmm, reads=["xfh", "xfl", "gw_hi", "gw_lo", "gb_hi",
                                    "gb_lo", "ones1", "psO"],
                   writes=["psO"], tag=f"p3 g {tile}.{tc}")
            P.step("V", lambda v, gsl=gsl: v.tensor_copy(g32[:, 0:NE_R],
                                                          psO[:, gsl]),
                   reads=["psO"], writes=["g32"], tag=f"p3 gc {tile}.{tc}")

            def gmax(v):
                v.max(m8, g32)
                v.drain()
                return v.tensor_scalar(gcol[:, 0:1], m8[:, 0:1], -1.0, None,
                                       ALU.mult)
            P.step("V", gmax, reads=["g32"], writes=["m8", "gcol"],
                   tag=f"p3 gm {tile}.{tc}")
            P.step("SC", lambda s: s.activation(eg, g32[:, 0:NE_R], AF.Exp,
                                                bias=gcol[:, 0:1],
                                                accum_out=gcol[:, 1:2]),
                   reads=["g32", "gcol"], writes=["eg", "gcol"],
                   tag=f"p3 ge {tile}.{tc}")

            def gnorm(v):
                v.reciprocal(mcol[:, 6:7], gcol[:, 1:2])
                v.drain()
                return v.tensor_scalar(w4[:, 0:NE_R], eg, mcol[:, 6:7], None,
                                       ALU.mult)
            P.step("V", gnorm, reads=["eg", "gcol"], writes=["w4", "mcol"],
                   tag=f"p3 gn {tile}.{tc}")

            def top2(v, tc=tc):
                v.max(m8, w4)
                v.drain()
                v.max_index(mi8, m8, w4)
                v.drain()
                v.tensor_copy(mif, mi8)
                v.drain()
                v.tensor_scalar(oh1, iotaF[:, 0:NE_R], mif[:, 0:1], None,
                                ALU.is_equal)
                v.tensor_scalar(oh2, iotaF[:, 0:NE_R], mif[:, 1:2], None,
                                ALU.is_equal)
                v.drain()
                v.tensor_tensor(oh1, oh1, oh2, ALU.add)
                v.drain()
                v.tensor_tensor(c4, w4[:, 0:NE_R], oh1, ALU.mult)
                v.drain()
                return v.tensor_copy(c4all[:, tc * NE_R:(tc + 1) * NE_R], c4)
            P.step("V", top2, reads=["w4", "iotaF"],
                   writes=["c4", "c4all", "m8", "mi8", "mif", "oh1", "oh2"],
                   tag=f"p3 t2 {tile}.{tc}")
            if DBG and tile == 0 and tc == 0:
                dma("GD", dbgG[:, 0:4], g32[:, 0:4], ["g32"], ["dbgG"], "dbg g1")
                dma("GD", dbgG[:, 4:8], w4[:, 0:4], ["w4"], ["dbgG"], "dbg g2")
                dma("GD", dbgG[:, 8:12], c4, ["c4"], ["dbgG"], "dbg g3")
                dma("GD", dbgG[:, 12:14], mif[:, 0:2], ["mif"], ["dbgG"], "dbg g4")
            P.step("V", lambda v: v.tensor_copy(c4h, c4), reads=["c4"],
                   writes=["c4h"], tag=f"p3 ch {tile}.{tc}")
            P.step("V", lambda v: v.scalar_tensor_tensor(
                c4l, c4, 1.0, c4h, ALU.mult, ALU.subtract),
                reads=["c4", "c4h"], writes=["c4l"], tag=f"p3 cl {tile}.{tc}")
            P.step("T", lambda te: te.transpose(psT[0:NE_R, 0:128], c4h, identbf),
                   reads=["c4h", "identbf", "psT"], writes=["psT"],
                   tag=f"p3 ct {tile}.{tc}")
            P.step("V", lambda v, tc=tc: v.tensor_copy(
                c6h[0:NE_R, tc * 128:(tc + 1) * 128], psT[0:NE_R, 0:128]),
                reads=["psT"], writes=["c6h"], tag=f"p3 ce {tile}.{tc}")
            P.step("T", lambda te: te.transpose(psT[0:NE_R, 0:128], c4l, identbf),
                   reads=["c4l", "identbf", "psT"], writes=["psT"],
                   tag=f"p3 ct2 {tile}.{tc}")
            P.step("V", lambda v, tc=tc: v.tensor_copy(
                c6l[0:NE_R, tc * 128:(tc + 1) * 128], psT[0:NE_R, 0:128]),
                reads=["psT"], writes=["c6l"], tag=f"p3 ce2 {tile}.{tc}")


        for ex in (DBG_EXPERTS if DBG_EXPERTS else range(NEX)):
            shared = ex < NE_S
            w1d = sw1_d if shared else rw1_d
            w2d = sw2_d if shared else rw2_d
            ei = ex if shared else ex - NE_S
            for fb in range(NFB):
                def emit_w(fi):
                    fc = fb * FB + fi
                    fsl = slice(fc * 128, (fc + 1) * 128)
                    slot = fi % 3
                    w1s32 = AR5[:, slot * 1024:slot * 1024 + 1024]
                    w1v = w1s32.rearrange("p (dc f) -> p dc f", dc=NDC)
                    src1 = w1d[ei].rearrange("(dc p) f -> p dc f", p=128)[:, :, fsl]
                    h1 = W1h[:, slot * 1024:slot * 1024 + 1024]
                    l1 = W1l[:, slot * 1024:slot * 1024 + 1024]
                    nm1 = f"w1s{slot}"
                    dma("SY", w1v, src1, ["i", nm1], [nm1],
                        f"p3 w1 {tile}.{ex}.{fc}")
                    P.step("SC", lambda s, a=h1, b=w1s32: s.activation(
                        a, b, AF.Copy), reads=[nm1, nm1 + "h"],
                        writes=[nm1 + "h"], tag=f"p3 w1h {tile}.{ex}.{fc}")
                    P.step("V", lambda v, a=l1, b=w1s32, c=h1:
                           v.scalar_tensor_tensor(a, b, 1.0, c, ALU.mult,
                                                  ALU.subtract),
                           reads=[nm1, nm1 + "h", nm1 + "l"], writes=[nm1 + "l"],
                           tag=f"p3 w1l {tile}.{ex}.{fc}")
                    w2s32 = AR6[:, slot * 1024:slot * 1024 + 1024]
                    src2 = w2d[ei][fsl, :]
                    h2 = W2h[:, fi * 1024:(fi + 1) * 1024]
                    l2 = W2l[:, fi * 1024:(fi + 1) * 1024]
                    nm2 = f"w2s{slot}"
                    dma("SY", w2s32, src2, ["i", nm2], [nm2],
                        f"p3 w2 {tile}.{ex}.{fc}")
                    P.step("SC", lambda s, a=h2, b=w2s32: s.activation(
                        a, b, AF.Copy), reads=[nm2, "W2h"], writes=["W2h"],
                        tag=f"p3 w2h {tile}.{ex}.{fc}")
                    P.step("V", lambda v, a=l2, b=w2s32, c=h2:
                           v.scalar_tensor_tensor(a, b, 1.0, c, ALU.mult,
                                                  ALU.subtract),
                           reads=[nm2, "W2h", "W2l"], writes=["W2l"],
                           tag=f"p3 w2l {tile}.{ex}.{fc}")

                def emit_h(fi):
                    fc = fb * FB + fi
                    slot = fi % 3
                    sp = fi % 2
                    h1 = W1h[:, slot * 1024:slot * 1024 + 1024]
                    l1 = W1l[:, slot * 1024:slot * 1024 + 1024]
                    h1v = h1.rearrange("p (dc f) -> p dc f", dc=NDC)
                    l1v = l1.rearrange("p (dc f) -> p dc f", dc=NDC)
                    nm1 = f"w1s{slot}"
                    ps, psn = (psA, "psA") if sp == 0 else (psB, "psB")

                    def hmm(te, h1v=h1v, l1v=l1v, ps=ps):
                        last = None
                        for dc in range(NDC):
                            tsl = slice(dc * SW, (dc + 1) * SW)
                            last = te.matmul(ps[:, 0:SW], h1v[:, dc, :],
                                             xfh[:, tsl],
                                             start=(dc == 0), stop=False)
                            last = te.matmul(ps[:, 0:SW], h1v[:, dc, :],
                                             xfl[:, tsl],
                                             start=False, stop=False)
                            last = te.matmul(ps[:, 0:SW], l1v[:, dc, :],
                                             xfh[:, tsl],
                                             start=False, stop=(dc == NDC - 1))
                        return last
                    P.step("T", hmm, reads=[nm1 + "h", nm1 + "l", "xfh", "xfl"],
                           writes=[psn], tag=f"p3 h {tile}.{ex}.{fc}")
                    bcol = b1cols[:, ex * NFC + fc:ex * NFC + fc + 1]
                    P.step("SC", lambda s, ps=ps, bcol=bcol, slot=sp,
                           sh=shared: s.activation(
                               hr32[slot][:, 0:SW], ps[:, 0:SW], AF.Relu,
                               bias=bcol, scale=(0.5 if sh else 1.0)),
                           reads=[psn, "b1cols"], writes=[f"E32_{sp}"],
                           tag=f"p3 hr {tile}.{ex}.{fc}")
                    hhs = hh_t[:, fi * SW:(fi + 1) * SW]
                    hls = hl_t[:, fi * SW:(fi + 1) * SW]
                    P.step("V", lambda v, a=hhs, slot=sp: v.tensor_copy(
                        a, hr32[slot][:, 0:SW]),
                        reads=[f"E32_{sp}", "hh_t"], writes=["hh_t"],
                        tag=f"p3 hh {tile}.{ex}.{fc}")
                    P.step("V", lambda v, a=hls, b=hhs, slot=sp:
                           v.scalar_tensor_tensor(a, hr32[slot][:, 0:SW], 1.0, b,
                                                  ALU.mult, ALU.subtract),
                           reads=[f"E32_{sp}", "hh_t", "hl_t"],
                           writes=["hl_t"], tag=f"p3 hl {tile}.{ex}.{fc}")

                for it in range(FB + 2):
                    if it < FB:
                        emit_w(it)
                    if it >= 2:
                        emit_h(it - 2)
                for tc in range(NTC):
                    for ot in range(NOT):
                        i = tc * NOT + ot
                        ps, psn = (psC, "psC") if i % 2 == 0 else (psD, "psD")

                        def ymm(te, tc=tc, ot=ot, ps=ps):
                            last = None
                            for fi in range(FB):
                                hsl = slice(fi * SW + tc * 128,
                                            fi * SW + tc * 128 + 128)
                                w2hs = W2h[:, fi * 1024 + ot * 512:
                                           fi * 1024 + ot * 512 + 512]
                                w2ls = W2l[:, fi * 1024 + ot * 512:
                                           fi * 1024 + ot * 512 + 512]
                                last = te.matmul(ps, hh_t[:, hsl], w2hs,
                                                 start=(fi == 0), stop=False)
                                last = te.matmul(ps, hl_t[:, hsl], w2hs,
                                                 start=False, stop=False)
                                last = te.matmul(ps, hh_t[:, hsl], w2ls,
                                                 start=False, stop=(fi == FB - 1))
                            return last
                        P.step("T", ymm, reads=["hh_t", "hl_t", "W2h", "W2l", psn],
                               writes=[psn], tag=f"p3 y {tile}.{ex}.{fb}.{i}")
                        if DBG and tile == 0 and ex == DBG_YEX and fb == 0 and i == 0:
                            P.step("SC", lambda s, ps=ps: s.activation(
                                dbgYs, ps, AF.Copy), reads=[psn],
                                writes=["dbgYs"], tag="dbg yev")
                            dma("GD", dbgY, dbgYs, ["dbgYs"], ["dbgY"], "dbg y")
                        xsl = slice(tc * D + ot * 512, tc * D + ot * 512 + 512)
                        if shared:
                            P.step("V", lambda v, ps=ps, xsl=xsl: v.tensor_tensor(
                                x1t[:, xsl], ps, x1t[:, xsl], ALU.add),
                                reads=[psn, "x1t"], writes=["x1t"],
                                tag=f"p3 ya {tile}.{ex}.{fb}.{i}")
                        else:
                            ccol = c4all[:, tc * NE_R + ei:tc * NE_R + ei + 1]
                            P.step("V", lambda v, ps=ps, xsl=xsl, ccol=ccol:
                                   v.scalar_tensor_tensor(
                                       x1t[:, xsl], ps, ccol, x1t[:, xsl],
                                       ALU.mult, ALU.add),
                                   reads=[psn, "x1t", "c4all"], writes=["x1t"],
                                   tag=f"p3 ya {tile}.{ex}.{fb}.{i}")

        for tc in range(NTC):
            for ot in range(NOT):
                i = tc * NOT + ot
                ps, psn = (psC, "psC") if i % 2 == 0 else (psD, "psD")

                def b2mm(te, tc=tc, ot=ot, ps=ps):
                    csl = slice(tc * 128, (tc + 1) * 128)
                    osl = slice(ot * 512, (ot + 1) * 512)
                    te.matmul(ps, c6h[:, csl], b2_hi[:, osl], start=True,
                              stop=False)
                    te.matmul(ps, c6l[:, csl], b2_hi[:, osl], start=False,
                              stop=False)
                    return te.matmul(ps, c6h[:, csl], b2_lo[:, osl],
                                     start=False, stop=True)
                P.step("T", b2mm, reads=["c6h", "c6l", "b2_hi", "b2_lo", psn],
                       writes=[psn], tag=f"p3 b2 {tile}.{i}")
                xsl = slice(tc * D + ot * 512, tc * D + ot * 512 + 512)
                P.step("V", lambda v, ps=ps, xsl=xsl: v.tensor_tensor(
                    x1t[:, xsl], ps, x1t[:, xsl], ALU.add),
                    reads=[psn, "x1t"], writes=["x1t"], tag=f"p3 b2a {tile}.{i}")

        for tc in range(NTC):
            rows = slice(trow + tc * 128, trow + (tc + 1) * 128)
            dma("GD", y_d[rows, :], x1t[:, tc * D:(tc + 1) * D],
                ["x1t"], [f"y_dram#{tile}.{tc}"], f"p3 out {tile}.{tc}")

    P.step("G", lambda g: g.memset(dummy, 0.0), reads=["y_dram"],
           writes=["dummy"], tag="final join")

    with nc.Block() as block:
        P.emit(block)
    return nc


_CACHE = {}


def _get_program(cfg_key):
    if cfg_key not in _CACHE:
        _CACHE[cfg_key] = build_program(Cfg())
    return _CACHE[cfg_key]


def kernel(L0_attn_x, L0_attn_wq, L0_attn_wk, L0_attn_wv, L0_attn_wo,
           L0_attn_wh, L0_ffn_gate_w, L0_ffn_gate_b, L0_ffn_sw1, L0_ffn_sb1,
           L0_ffn_sw2, L0_ffn_sb2, L0_ffn_rw1, L0_ffn_rb1, L0_ffn_rw2,
           L0_ffn_rb2):
    cfg = Cfg()
    x = np.ascontiguousarray(np.asarray(L0_attn_x, dtype=np.float32))
    Bn, TF, _ = x.shape
    n_cores = 8
    nc = _get_program("full")

    common = {
        "wq": np.ascontiguousarray(np.asarray(L0_attn_wq, np.float32)),
        "wk": np.ascontiguousarray(np.asarray(L0_attn_wk, np.float32)),
        "wv": np.ascontiguousarray(np.asarray(L0_attn_wv, np.float32)),
        "wo": np.ascontiguousarray(np.asarray(L0_attn_wo, np.float32)),
        "wh": np.ascontiguousarray(np.asarray(L0_attn_wh, np.float32)),
        "gate_w": np.ascontiguousarray(np.asarray(L0_ffn_gate_w, np.float32)),
        "gate_b": np.ascontiguousarray(np.asarray(L0_ffn_gate_b, np.float32)).reshape(1, NE_R),
        "sw1": np.ascontiguousarray(np.asarray(L0_ffn_sw1, np.float32)),
        "sb1": np.ascontiguousarray(np.asarray(L0_ffn_sb1, np.float32)),
        "sw2": np.ascontiguousarray(np.asarray(L0_ffn_sw2, np.float32)),
        "sb2": np.ascontiguousarray(np.asarray(L0_ffn_sb2, np.float32)),
        "rw1": np.ascontiguousarray(np.asarray(L0_ffn_rw1, np.float32)),
        "rb1": np.ascontiguousarray(np.asarray(L0_ffn_rb1, np.float32)),
        "rw2": np.ascontiguousarray(np.asarray(L0_ffn_rw2, np.float32)),
        "rb2": np.ascontiguousarray(np.asarray(L0_ffn_rb2, np.float32)),
    }
    in_maps = []
    rows_per_core = []
    for c in range(n_cores):
        b, p = divmod(c, 2)
        stripe_ids = [2 * jj + p for jj in range(cfg.NSTR)]
        own = np.concatenate([np.arange(s * cfg.SW, (s + 1) * cfg.SW)
                              for s in stripe_ids])
        rows_per_core.append((b, own))
        m = dict(common)
        m["x_full"] = x[b]
        m["x_own"] = np.ascontiguousarray(x[b][own])
        m["qpos"] = np.ascontiguousarray(own.astype(np.float32)).reshape(1, -1)
        in_maps.append(m)

    res = run_bass_kernel_spmd(nc, in_maps, core_ids=list(range(n_cores)))
    out = np.zeros_like(x)
    for c in range(n_cores):
        b, own = rows_per_core[c]
        out[b][own] = np.asarray(res.results[c]["y"])
    return out
